# version 12
# baseline (speedup 1.0000x reference)
import contextlib
import math
from types import SimpleNamespace

import numpy as np

import concourse.bass as bass
import concourse.mybir as mybir
from concourse.bass_utils import run_bass_kernel_spmd

F32 = mybir.dt.float32
BF16 = mybir.dt.bfloat16
AF = mybir.ActivationFunctionType
ALU = mybir.AluOpType
AXX = mybir.AxisListType.X

D = 128
TILE = 512
NCORES = 8


def make_cfg(E=2048, T=2048, B=2, DFF=8192, NKV=4, NE=8, DM=1408):
    c = SimpleNamespace()
    c.E, c.T, c.B, c.DFF, c.NKV, c.NE, c.DM = E, T, B, DFF, NKV, NE, DM
    c.KC = E // D
    c.H = E // D
    c.G = c.H // NKV
    c.CKV = NKV * D
    c.KCV = c.CKV // D
    c.TQ = T // TILE
    c.NT = c.B * c.TQ
    c.KFF = DFF // D
    c.KM = DM // D
    c.scale = 1.0 / math.sqrt(D)
    c.eps = 1e-5
    return c


PHASES_ALL = ["p0", "p1", "p2", "p3", "p4", "p5", "p6"]

PHASE_W = {
    "p1": ["q0", "k0", "v0"],
    "p3": ["o0", "gu0", "dn0"],
    "p4": ["q1", "k1", "v1"],
    "p6": ["o1", "e1", "e2", "e3"],
}
PHASE_IN = {
    "p1": ["Xin"], "p2": ["Q0", "K0", "V0", "NB"], "p3": ["Xin", "O0"],
    "p4": ["X2"], "p5": ["QE1", "K1", "V1", "NB"], "p6": ["X2", "O1"],
}
PHASE_OUT = {
    "p1": ["Q0", "K0", "V0", "NB"], "p2": ["O0"], "p3": ["X2"],
    "p4": ["QE1", "K1", "V1", "NB"], "p5": ["O1"], "p6": ["OUT"],
}



def weight_plan(K, M, colq=TILE, kcmax=16):
    KCn = K // D
    units = []
    off = 0
    for col0 in range(0, M, colq):
        ncols = min(colq, M - col0)
        for kc0 in range(0, KCn, kcmax):
            nkc = min(kcmax, KCn - kc0)
            units.append(SimpleNamespace(off=off, kc0=kc0, nkc=nkc,
                                         col0=col0, ncols=ncols))
            off += nkc * ncols
    return units, off


def host_block_weight(w, colq=TILE):
    w = np.asarray(w, dtype=np.float32)
    K, M = w.shape
    units, tot = weight_plan(K, M, colq=colq)
    out = np.empty((D, tot), dtype=np.float32)
    for u in units:
        sub = w[u.kc0 * D:(u.kc0 + u.nkc) * D, u.col0:u.col0 + u.ncols]
        blk = sub.reshape(u.nkc, D, u.ncols).transpose(1, 0, 2).reshape(D, -1)
        out[:, u.off:u.off + u.nkc * u.ncols] = blk
    return out


def host_block_x(xc):
    B, T, E = xc.shape
    y = np.asarray(xc, dtype=np.float32).reshape(B, T // TILE, TILE, E // D, D)
    return np.ascontiguousarray(y.transpose(0, 1, 4, 3, 2))


def host_unblock_x(y):
    B, TQ, P, KC, C = y.shape
    return np.ascontiguousarray(
        y.transpose(0, 1, 4, 3, 2).reshape(B, TQ * C, KC * P))


def rope_perm(E):
    perm = np.arange(E).reshape(-1, D)
    half = np.concatenate([np.arange(0, D, 2), np.arange(1, D, 2)])
    return (perm[:, half]).reshape(-1)


def host_consts(cfg):
    T = cfg.T
    pos = np.arange(T, dtype=np.float64)
    inv = 10000.0 ** (-np.arange(64, dtype=np.float64) / 64.0)
    ang = inv[:, None] * pos[None, :]
    cos = np.cos(ang).astype(np.float32)
    sin = np.sin(ang).astype(np.float32)
    cos2 = np.concatenate([cos, cos], axis=0)
    sin2 = np.concatenate([sin, sin], axis=0)
    R = np.zeros((D, D), dtype=np.float32)
    R[0:64, 64:128] = -np.eye(64)
    R[64:128, 0:64] = np.eye(64)
    RT = np.ascontiguousarray(R.T)
    masks = np.zeros((4, D, TILE), dtype=np.float32)
    for j in range(4):
        kk = np.arange(D)[:, None] + D * j
        qq = np.arange(TILE)[None, :]
        masks[j] = (kk <= qq).astype(np.float32)
    ident = np.eye(D, dtype=np.float32)
    E16 = np.zeros((D, cfg.KC * cfg.KC), dtype=np.float32)
    for kc in range(cfg.KC):
        E16[:, kc * cfg.KC + kc] = 1.0
    E4 = np.zeros((D, cfg.KCV * cfg.KCV), dtype=np.float32)
    for kc in range(cfg.KCV):
        E4[:, kc * cfg.KCV + kc] = 1.0
    return dict(cos=cos2, sin=sin2, ropeRT=RT,
                masks=masks.reshape(4 * D, TILE), ident=ident, E16=E16, E4=E4)



class Sem:
    def __init__(self, stack, nc, name):
        self.h = stack.enter_context(nc.semaphore(name))
        self.n = 0
        self.name = name

    def inc(self, instr, k=1):
        instr.then_inc(self.h, k)
        self.n += k
        return self.n


class MK:
    def __init__(self, nc, cfg, stack):
        self.nc = nc
        self.cfg = cfg
        self.stack = stack
        self.dram = {}
        self.sb = {}
        self.ps = []
        self.sems = {}
        self._waited = {}
        self._last_emitter = {}
        self._promises = []
        self.wmeta = {}
        self.attn_state = {}

    def sem(self, name):
        if name not in self.sems:
            self.sems[name] = Sem(self.stack, self.nc, name)
        return self.sems[name]

    SELF = {"T": "sT", "V": "sV", "S": "sS", "P": "sP"}

    def wait(self, proxy, who, sem, val, future=False, force=False):
        if val <= 0:
            return
        key = (who, sem.name)
        force = force or (self.SELF.get(who) == sem.name)
        if not force and self._waited.get(key, -1) >= val:
            return
        if force and self._waited.get(key, -1) >= val and \
                self._last_emitter.get(who) == key:
            return
        if not future:
            assert val <= sem.n, \
                f"wait {who} {sem.name}>={val} but only {sem.n} produced"
        else:
            self._promises.append((who, sem.name, val))
        proxy.wait_ge(sem.h, val)
        self._waited[key] = val
        self._last_emitter[who] = key

    def wait_all(self, proxy, who, names=None):
        for nm, s in list(self.sems.items()):
            if names is not None and nm not in names:
                continue
            self.wait(proxy, who, s, s.n, force=True)

    def check_promises(self):
        for who, semname, val in self._promises:
            s = self.sems[semname]
            assert val <= s.n, f"promised {who} {semname}>={val}, produced {s.n}"
        self._promises = []

    def dram_t(self, name, shape, dtype, kind):
        h = self.nc.dram_tensor(name, list(shape), dtype, kind=kind)
        self.dram[name] = h
        return h

    def sbuf(self, name, shape, dtype):
        h = self.stack.enter_context(self.nc.sbuf_tensor(name, list(shape), dtype))
        self.sb[name] = h
        return h

    def psum_init(self):
        self.psp = []
        self.psp.append(self.stack.enter_context(
            self.nc.psum_tensor("pspA", [D, 2 * TILE], F32)))
        for i in range(2, 6):
            self.ps.append(self.stack.enter_context(
                self.nc.psum_tensor(f"psb{i}", [D, TILE], F32)))
        self.psp.append(self.stack.enter_context(
            self.nc.psum_tensor("pspB", [D, 2 * TILE], F32)))
        pa, pb = self.psp
        self.ps = [pa[:, 0:TILE], pa[:, TILE:2 * TILE]] + \
            [t[:, :] for t in self.ps] + \
            [pb[:, 0:TILE], pb[:, TILE:2 * TILE]]


def wspecs(cfg):
    E = cfg.E
    return {
        "q0": (E, E, 1), "k0": (E, E, 1), "v0": (E, E, 1), "o0": (E, E, 1),
        "gu0": (E, 2 * cfg.DFF, 1), "dn0": (cfg.DFF, E, 1),
        "q1": (E, E, 1), "k1": (E, cfg.CKV, 1), "v1": (E, cfg.CKV, 1),
        "o1": (cfg.CKV, E, 1),
        "e1": (E, cfg.DM, cfg.NE), "e2": (E, cfg.DM, cfg.NE),
        "e3": (cfg.DM, E, cfg.NE),
    }


def declare_weights(mk, names, need_g1, need_bias):
    cfg = mk.cfg
    allspecs = wspecs(cfg)
    for name in names:
        K, M, nrep = allspecs[name]
        units, tot = weight_plan(K, M)
        mk.wmeta[name] = (units, tot, nrep)
        mk.dram_t("wf_" + name, (D, nrep * tot), F32, "ExternalInput")
        mk.dram_t("wb_" + name, (D, nrep * tot), BF16, "Internal")
    if need_g1:
        units, tot = weight_plan(cfg.E, cfg.NE, colq=cfg.NE)
        mk.wmeta["g1"] = (units, tot, 1)
        mk.dram_t("wf_g1", (D, tot), F32, "ExternalInput")
    if need_bias:
        for name, n in (("bq", cfg.E), ("bk", cfg.CKV), ("bv", cfg.CKV),
                        ("bo", cfg.E)):
            mk.dram_t("bias_" + name, (1, n), F32, "ExternalInput")


def scratch_shapes(cfg):
    return {
        "Xin": ((cfg.B, cfg.TQ, D, cfg.KC, TILE), F32),
        "X2": ((cfg.B, cfg.TQ, D, cfg.KC, TILE), F32),
        "X3": ((cfg.B, cfg.TQ, D, cfg.KC, TILE), F32),
        "OUT": ((cfg.B, cfg.TQ, D, cfg.KC, TILE), F32),
        "Q0": ((cfg.B, cfg.H, D, cfg.T), BF16),
        "K0": ((cfg.B, cfg.H, D, cfg.T), BF16),
        "V0": ((cfg.B, cfg.H, cfg.T, D), BF16),
        "O0": ((cfg.B, cfg.H, D, cfg.T), BF16),
        "QE1": ((cfg.B, cfg.NKV, D, cfg.T), BF16),
        "K1": ((cfg.B, cfg.NKV, D, cfg.T), BF16),
        "V1": ((cfg.B, cfg.NKV, cfg.T, D), BF16),
        "O1": ((cfg.B, cfg.NKV, D, cfg.T), BF16),
        "NB": ((2, cfg.B), F32),
    }


def declare_scratch(mk, names, kinds):
    shapes = scratch_shapes(mk.cfg)
    for n in names:
        sh, dt = shapes[n]
        mk.dram_t(n, sh, dt, kinds.get(n, "Internal"))


def declare_consts(mk):
    for name, arr in host_consts(mk.cfg).items():
        mk.dram["c_" + name] = mk.nc.inline_tensor(
            np.ascontiguousarray(arr), "c_" + name)


AR_N = 68


def ar(mk, t0, nt=1):
    return mk.sb["arena"][:, t0 * TILE:(t0 + nt) * TILE]


def declare_sbuf(mk):
    cfg = mk.cfg
    KCT = cfg.KC * TILE
    mk.sbuf("xhat", [D, KCT], BF16)
    mk.sbuf("Xres", [D, KCT], F32)
    mk.sbuf("arena", [D, AR_N * TILE], BF16)
    mk.sbuf("wsA", [D, 8192], BF16)
    mk.sbuf("wsB", [D, 8192], BF16)
    mk.sbuf("ones_row", [1, TILE], BF16)
    mk.sbuf("ones_col", [D, 1], BF16)
    mk.sbuf("onesf", [D, 1], F32)
    mk.sbuf("ident_f", [D, D], F32)
    mk.sbuf("ropeRT", [D, D], BF16)
    mk.sbuf("masks", [D, 4 * TILE], BF16)
    mk.sbuf("cosb", [D, cfg.T], BF16)
    mk.sbuf("sinb", [D, cfg.T], BF16)
    mk.sbuf("E16", [D, cfg.KC * cfg.KC], BF16)
    mk.sbuf("E4", [D, cfg.KCV * cfg.KCV], BF16)
    mk.sbuf("brows", [1, 2 * cfg.E + 2 * cfg.CKV], BF16)
    mk.sbuf("wg", [D, cfg.KC * cfg.NE], F32)
    mk.sbuf("csum", [1, cfg.NE], F32)
    mk.sbuf("stats", [1, 5 * TILE], F32)
    mk.sbuf("rowsb", [1, 2 * TILE], BF16)
    mk.sbuf("nrm", [cfg.KC, 4], F32)
    mk.sbuf("negB", [D, 2], F32)
    mk.sbuf("smal", [D, 128], F32)
    mk.sbuf("g8", [cfg.NE, TILE], F32)
    mk.sbuf("g8s", [cfg.NE, TILE], F32)
    mk.sbuf("wBe", [D, TILE], BF16)
    mk.sbuf("fstage", [D, TILE], F32)
    mk.sbuf("fstage2", [D, TILE], F32)



def emit_fence(mk, blk):
    sems = list(mk.sems.values())
    fence = [(s, s.n) for s in sems]

    def mkf(who):
        def f(e):
            for s, v in fence:
                mk.wait(e, who, s, v)
        return f
    blk.tensor(mkf("T"))
    blk.vector(mkf("V"))
    blk.scalar(mkf("S"))
    blk.gpsimd(mkf("P"))
    blk.sync(mkf("SY"))


def emit_startup(mk, blk, phases):
    cfg = mk.cfg
    sG = mk.sem("sG")
    sV = mk.sem("sV")
    sT = mk.sem("sT")
    c, sb = mk.dram, mk.sb

    def f_sync(sync):
        mk.sem("sSY").inc(
            sync.dma_start(out=sb["ident_f"][:], in_=c["c_ident"].ap()), 16)
    blk.sync(f_sync)

    def f_g(g):
        casts = [
            (sb["identb"][:], c["c_ident"].ap()),
            (sb["ropeRT"][:], c["c_ropeRT"].ap()),
            (sb["masks"][:].rearrange("p (j t) -> p j t", j=4),
             c["c_masks"].ap().rearrange("(j p) t -> p j t", p=D)),
            (sb["cosb"][:], c["c_cos"].ap()),
            (sb["sinb"][:], c["c_sin"].ap()),
            (sb["E16"][:], c["c_E16"].ap()),
            (sb["E4"][:], c["c_E4"].ap()),
        ]
        if "bias_bq" in mk.dram:
            casts += [
                (sb["brows"][0:1, 0:cfg.E], c["bias_bq"].ap()),
                (sb["brows"][0:1, cfg.E:2 * cfg.E], c["bias_bo"].ap()),
                (sb["brows"][0:1, 2 * cfg.E:2 * cfg.E + cfg.CKV], c["bias_bk"].ap()),
                (sb["brows"][0:1, 2 * cfg.E + cfg.CKV:], c["bias_bv"].ap()),
            ]
        for o, i in casts:
            sG.inc(g.dma_start(out=o, in_=i), 16)
        if "wf_g1" in mk.dram:
            sG.inc(g.dma_start(out=sb["wg"][:], in_=c["wf_g1"].ap()), 16)
    blk.gpsimd(f_g)

    def f_v(v):
        sV.inc(v.memset(sb["ones_row"][:], 1.0))
        sV.inc(v.memset(sb["ones_col"][:], 1.0))
        sV.inc(v.memset(sb["onesf"][:], 1.0))
        sV.inc(v.memset(sb["nrm"][:], 0.0))
    blk.vector(f_v)

    if "wf_g1" in mk.dram:
        gtok, vtok = sG.n, sV.n

        def f_t(t):
            mk.wait(t, "T", sG, gtok)
            mk.wait(t, "T", sV, vtok)
            for kc in range(cfg.KC):
                i = t.matmul(mk.ps[6][0:1, 0:cfg.NE],
                             sb["onesf"][:],
                             sb["wg"][:, kc * cfg.NE:(kc + 1) * cfg.NE],
                             start=(kc == 0), stop=(kc == cfg.KC - 1))
            sT.inc(i)
        blk.tensor(f_t)
        ttok = sT.n

        def f_v2(v):
            mk.wait(v, "V", sT, ttok)
            sV.inc(v.tensor_copy(out=sb["csum"][:],
                                 in_=mk.ps[6][0:1, 0:cfg.NE]))
        blk.vector(f_v2)


def emit_p0(mk, blk, names):
    CH = 1 << 15

    def f(g):
        for name in names:
            if name == "g1" or name not in mk.wmeta:
                continue
            sW = mk.sem("sWc_" + name)
            units, tot, nrep = mk.wmeta[name]
            total = tot * nrep
            srcap = mk.dram["wf_" + name].ap()
            dstap = mk.dram["wb_" + name].ap()
            for o0 in range(0, total, CH):
                o1 = min(total, o0 + CH)
                sW.inc(g.dma_start(out=dstap[:, o0:o1], in_=srcap[:, o0:o1]), 16)
            mk.conv_marks[name] = (sW, sW.n)
    blk.gpsimd(f)



class WStream:
    def __init__(self, mk, blk):
        self.mk = mk
        self.blk = blk
        self.slots = [mk.sb["wsA"], mk.sb["wsB"]]
        self.ssem = [mk.sem("sWs0"), mk.sem("sWs1")]
        st = getattr(mk, "ws_state", None)
        if st is None:
            st = mk.ws_state = {"i": 0, "count": [0, 0], "consumed": [None, None]}
        self.i = st["i"]
        self.count = st["count"]
        self.consumed = st["consumed"]

    def _save(self):
        self.mk.ws_state["i"] = self.i

    def load(self, name, rep, u):
        mk = self.mk
        s = self.i % 2
        self.i += 1
        cons = self.consumed[s]
        _, tot, _ = mk.wmeta[name]
        base = rep * tot + u.off
        width = u.nkc * u.ncols
        apw = mk.dram["wb_" + name].ap()[:, base:base + width]
        conv = getattr(mk, "conv_marks", {}).get(name)

        def f(sync):
            if conv:
                mk.wait(sync, "SY", conv[0], conv[1])
            if cons is not None:
                mk.wait(sync, "SY", cons[0], cons[1])
            self.ssem[s].inc(
                sync.dma_start(out=self.slots[s][:, 0:width], in_=apw), 16)
        self.blk.sync(f)
        self.count[s] += 1
        self._save()
        return SimpleNamespace(buf=self.slots[s], slot=s, sem=self.ssem[s],
                               tok=16 * self.count[s], u=u)

    def mark_consumed(self, s, sem, val):
        self.consumed[s] = (sem, val)
        self._save()


class StageRing:
    def __init__(self, mk, blk, base, n):
        self.mk = mk
        self.blk = blk
        self.base = base
        self.n = n
        self.sems = [mk.sem(f"sStg{i}") for i in range(n)]
        self.uses = [0] * n
        self.i = 0

    def next(self):
        s = self.i % self.n
        self.i += 1
        return s

    def slot_ap(self, s):
        return ar(self.mk, self.base + s)

    def wait_free(self, proxy, who, s):
        if self.uses[s] > 0:
            self.mk.wait(proxy, who, self.sems[s], 16 * self.uses[s])

    def store(self, s, dst_ap, prod_sem, prod_tok):
        mk = self.mk

        def f(sc):
            mk.wait(sc, "SC", prod_sem, prod_tok)
            self.sems[s].inc(sc.dma_start(out=dst_ap, in_=self.slot_ap(s)), 16)
        self.blk.scalar(f)
        self.uses[s] += 1



def emit_ln(mk, blk, b, tq, src_dram, xbf_off, sq_off, xbf_ready=None):
    cfg = mk.cfg
    sG, sP = mk.sem("sG"), mk.sem("sP")
    sS, sV, sT = mk.sem("sS"), mk.sem("sV"), mk.sem("sT")
    sb = mk.sb
    xbf = ar(mk, xbf_off, cfg.KC)

    if src_dram is not None:
        prevu = getattr(mk, "xbf_prev_user", None)
        if prevu is None:
            prevu = [(sV, sV.n), (sT, sT.n), (sS, sS.n), (sP, sP.n)]
        sXF = [mk.sem("sXF0"), mk.sem("sXF1")]
        stage = [sb["fstage"], sb["fstage2"]]
        fs_user = getattr(mk, "fstage_user", None)
        ld_toks = {}

        vbase = sV.n

        def f_g(sync):
            for sm_, tk_ in prevu:
                mk.wait(sync, "SY", sm_, tk_)
            if fs_user is not None:
                mk.wait(sync, "SY", fs_user[0], fs_user[1])
            for kc in range(cfg.KC):
                if kc >= 2:
                    mk.wait(sync, "SY", sV, vbase + (kc - 2) + 1, future=True)
                sXF[kc % 2].inc(
                    sync.dma_start(out=stage[kc % 2][:],
                                   in_=mk.dram[src_dram].ap()[b, tq, :, kc, :]), 16)
                ld_toks[kc] = sXF[kc % 2].n
        cast_toks = {}

        def f_cst(v):
            mk.wait(v, "V", sV, sV.n)
            for kc in range(cfg.KC):
                mk.wait(v, "V", sXF[kc % 2], ld_toks[kc])
                sV.inc(v.tensor_copy(out=xbf[:, kc * TILE:(kc + 1) * TILE],
                                     in_=stage[kc % 2][:]))
                cast_toks[kc] = sV.n
        blk.sync(f_g)
        blk.vector(f_cst)
        mk.fstage_user = (sV, sV.n)
        xbf_ready = (sV, sV.n)

    vtok0, ttok0 = sV.n, sT.n
    sqV, sqP = {}, {}

    def f_sq_v(v):
        mk.wait(v, "V", xbf_ready[0], xbf_ready[1])
        mk.wait(v, "V", sT, ttok0)
        mk.wait(v, "V", sV, sV.n)
        for kc in range(0, cfg.KC, 2):
            sV.inc(v.tensor_tensor(ar(mk, sq_off + kc),
                                   xbf[:, kc * TILE:(kc + 1) * TILE],
                                   xbf[:, kc * TILE:(kc + 1) * TILE], ALU.mult))
            sqV[kc] = sV.n
    blk.vector(f_sq_v)

    def f_sq_p(p):
        mk.wait(p, "P", xbf_ready[0], xbf_ready[1])
        mk.wait(p, "P", sT, ttok0)
        mk.wait(p, "P", sV, vtok0)
        mk.wait(p, "P", sP, sP.n)
        for kc in range(1, cfg.KC, 2):
            sP.inc(p.tensor_tensor(ar(mk, sq_off + kc),
                                   xbf[:, kc * TILE:(kc + 1) * TILE],
                                   xbf[:, kc * TILE:(kc + 1) * TILE], ALU.mult))
            sqP[kc] = sP.n
    blk.gpsimd(f_sq_p)

    stok0 = sS.n

    def f_t(t):
        mk.wait(t, "T", xbf_ready[0], xbf_ready[1])
        mk.wait(t, "T", sS, stok0)
        for kc in range(cfg.KC):
            if kc % 2 == 0:
                mk.wait(t, "T", sV, sqV[kc])
            else:
                mk.wait(t, "T", sP, sqP[kc])
            t.matmul(mk.ps[6][0:1, :], mk.sb["ones_col"][:],
                     xbf[:, kc * TILE:(kc + 1) * TILE],
                     start=(kc == 0), stop=(kc == cfg.KC - 1))
            i = t.matmul(mk.ps[7][0:1, :], mk.sb["ones_col"][:],
                         ar(mk, sq_off + kc),
                         start=(kc == 0), stop=(kc == cfg.KC - 1))
        sT.inc(i)
    blk.tensor(f_t)
    ttok1 = sT.n

    stats = sb["stats"]
    mean = stats[0:1, 0:TILE]
    ex2 = stats[0:1, TILE:2 * TILE]
    var = stats[0:1, 2 * TILE:3 * TILE]
    rstd = stats[0:1, 3 * TILE:4 * TILE]
    brow = stats[0:1, 4 * TILE:5 * TILE]
    abf = sb["rowsb"][0:1, 0:TILE]
    bbf = sb["rowsb"][0:1, TILE:2 * TILE]
    rinv = 1.0 / cfg.E

    def f_v(v):
        mk.wait(v, "V", sT, ttok1)
        sV.inc(v.tensor_scalar_mul(mean, mk.ps[6][0:1, :], rinv))
        sV.inc(v.tensor_scalar_mul(ex2, mk.ps[7][0:1, :], rinv))
        mk.wait(v, "V", sV, sV.n)
        sV.inc(v.scalar_tensor_tensor(var, mean, -1.0, mean, ALU.mult, ALU.mult))
        mk.wait(v, "V", sV, sV.n)
        sV.inc(v.tensor_add(var, var, ex2))
        mk.wait(v, "V", sV, sV.n)
        sV.inc(v.tensor_scalar_add(var, var, cfg.eps))
    blk.vector(f_v)
    vtok1 = sV.n

    def f_s(s):
        mk.wait(s, "S", sV, vtok1)
        sS.inc(s.activation(out=var, in_=var, func=AF.Ln))
        mk.wait(s, "S", sS, sS.n)
        sS.inc(s.activation(out=rstd, in_=var, func=AF.Exp, scale=-0.5))
    blk.scalar(f_s)
    stok1 = sS.n

    def f_v2(v):
        mk.wait(v, "V", sS, stok1)
        sV.inc(v.scalar_tensor_tensor(brow, mean, -1.0, rstd, ALU.mult, ALU.mult))
        mk.wait(v, "V", sV, sV.n)
        sV.inc(v.tensor_copy(out=abf, in_=rstd))
        sV.inc(v.tensor_copy(out=bbf, in_=brow))
    blk.vector(f_v2)
    vtok2 = sV.n

    def f_t2(t):
        mk.wait(t, "T", sV, vtok2)
        t.matmul(mk.ps[6][:, :], mk.sb["ones_row"][0:1, 0:D], abf,
                 start=True, stop=True)
        sT.inc(t.matmul(mk.ps[7][:, :], mk.sb["ones_row"][0:1, 0:D], bbf,
                        start=True, stop=True))
    blk.tensor(f_t2)
    ttok2 = sT.n

    aB = ar(mk, 60)
    bB = ar(mk, 61)

    def f_s2(v):
        mk.wait(v, "V", sT, ttok2)
        mk.wait(v, "V", sP, sP.n)
        mk.wait(v, "V", sV, sV.n)
        sV.inc(v.tensor_copy(out=aB, in_=mk.ps[6][:, :]))
        sV.inc(v.tensor_copy(out=bB, in_=mk.ps[7][:, :]))
    blk.vector(f_s2)
    stok2 = sV.n

    chunk_toks = {}

    def f_v3(v):
        mk.wait(v, "V", sV, stok2)
        tmp = ar(mk, 62)
        for kc in range(cfg.KC):
            mk.wait(v, "V", sV, sV.n)
            i1 = v.tensor_tensor(tmp, xbf[:, kc * TILE:(kc + 1) * TILE],
                                 aB, ALU.mult)
            sV.inc(i1)
            mk.wait(v, "V", sV, sV.n)
            i2 = v.tensor_tensor(mk.sb["xhat"][:, kc * TILE:(kc + 1) * TILE],
                                 tmp, bB, ALU.add)
            sV.inc(i2)
            chunk_toks[kc] = sV.n
    blk.vector(f_v3)
    if src_dram is not None:
        mk.xbf_prev_user = [(sV, sV.n)]
    return chunk_toks, sV.n



def fm_proj(mk, blk, ws, wname, rep, rhs, rhs_sem, rhs_tok, kclist, mqs,
            evac, bias_row_off=None, psmode="ping", rhs_toks=None):
    cfg = mk.cfg
    sT = mk.sem("sT")
    units, tot, _ = mk.wmeta[wname]
    prev_free = [None, None]

    for gi, mq in enumerate(mqs):
        u = units[mq]
        assert u.nkc == len(kclist), (wname, mq, u.nkc, len(kclist))
        ld = ws.load(wname, rep, u)
        nm = (u.ncols + D - 1) // D
        gslot = (gi % 2) if psmode == "ping" else 0
        psbase = 4 * gslot
        pf = prev_free[gslot]

        def f_t(t, ld=ld, nm=nm, psbase=psbase, pf=pf, u=u, gi=gi):
            first = gi == 0 and rhs_toks is not None
            if not first:
                mk.wait(t, "T", rhs_sem, rhs_tok)
            mk.wait(t, "T", ld.sem, ld.tok)
            if pf is not None:
                mk.wait(t, "T", pf[0], pf[1])
            last = None
            nk = len(kclist)
            bias = bias_row_off is not None
            for ki, kc in enumerate(kclist):
                if first and ki % 4 == 0:
                    mk.wait(t, "T", rhs_sem,
                            rhs_toks[min(kc + 3, kclist[-1])])
                r = rhs(kc)
                for m in range(nm):
                    mc = min(D, u.ncols - m * D)
                    last = t.matmul(
                        mk.ps[psbase + m][:, 0:r.shape[-1]],
                        ld.buf[:, ki * u.ncols + m * D:ki * u.ncols + m * D + mc],
                        r, start=(ki == 0),
                        stop=(ki == nk - 1 and not bias))
            if bias:
                for m in range(nm):
                    mc = min(D, u.ncols - m * D)
                    co = bias_row_off + u.col0 + m * D
                    last = t.matmul(
                        mk.ps[psbase + m][:, 0:TILE],
                        mk.sb["brows"][0:1, co:co + mc],
                        mk.sb["ones_row"][0:1, :],
                        start=False, stop=True)
            sT.inc(last)
        blk.tensor(f_t)
        ttok = sT.n
        ws.mark_consumed(ld.slot, sT, ttok)
        prev_free[gslot] = evac(blk, mq, psbase, nm, ttok)



def emit_rope(mk, blk, raw_tok_sem, raw_tok, raw, tq, psidx, slot, eng="V"):
    cfg = mk.cfg
    sE = mk.sem("s" + eng)
    sT = mk.sem("sT")
    xc = ar(mk, 32 + 2 * slot)
    xs = ar(mk, 33 + 2 * slot)
    cs = mk.sb["cosb"][:, tq * TILE:(tq + 1) * TILE]
    sn = mk.sb["sinb"][:, tq * TILE:(tq + 1) * TILE]

    def f_e(e):
        mk.wait(e, eng, raw_tok_sem, raw_tok)
        mk.wait(e, eng, sT, mk._rope_last_t.get(slot, 0))
        sE.inc(e.tensor_tensor(xc, raw, cs, ALU.mult))
        sE.inc(e.tensor_tensor(xs, raw, sn, ALU.mult))
    (blk.vector if eng == "V" else blk.gpsimd)(f_e)
    etok = sE.n

    ps_free = mk._rope_ps_free.get(psidx)

    def f_t(t):
        mk.wait(t, "T", sE, etok)
        if ps_free is not None:
            mk.wait(t, "T", ps_free[0], ps_free[1])
        t.matmul(mk.ps[psidx][:, :], mk.sb["ropeRT"][:, :], xs,
                 start=True, stop=False)
        sT.inc(t.matmul(mk.ps[psidx][:, :], mk.sb["identb"][:, :], xc,
                        start=False, stop=True))
    blk.tensor(f_t)
    mk._rope_last_t[slot] = sT.n
    return sT.n



def emit_norms(mk, blk, col, src_base, nch):
    cfg = mk.cfg
    sT, sV, sP = mk.sem("sT"), mk.sem("sV"), mk.sem("sP")
    nE = cfg.KC if nch == cfg.KC else cfg.KCV
    Emat = mk.sb["E16"] if nch == cfg.KC else mk.sb["E4"]
    ptok0 = sP.n
    tbase = sT.n
    vtok0 = sV.n

    def f_p(p):
        mk.wait(p, "P", sV, vtok0)
        for i in range(nch):
            if i >= 2:
                mk.wait(p, "P", sT, tbase + (i - 1), future=True)
            src = ar(mk, src_base + i)
            sP.inc(p.tensor_tensor(ar(mk, 62 + (i % 2)), src, src, ALU.mult))
    blk.gpsimd(f_p)

    vcur0 = sV.n

    def f_t(t):
        mk.wait(t, "T", sV, vcur0)
        for i in range(nch):
            mk.wait(t, "T", sP, ptok0 + i + 1)
            sT.inc(t.matmul(mk.ps[2][0:nE, :],
                            Emat[:, i * nE:(i + 1) * nE],
                            ar(mk, 62 + (i % 2)),
                            start=(i == 0), stop=(i == nch - 1)))
    blk.tensor(f_t)
    ttok = sT.n

    def f_v(v):
        mk.wait(v, "V", sT, ttok)
        sV.inc(v.reduce_max(mk.sb["smal"][0:nE, 32:33], mk.ps[2][0:nE, :],
                            axis=AXX))
        mk.wait(v, "V", sV, sV.n)
        sV.inc(v.tensor_tensor(mk.sb["nrm"][0:nE, col:col + 1],
                               mk.sb["nrm"][0:nE, col:col + 1],
                               mk.sb["smal"][0:nE, 32:33], ALU.max))
    blk.vector(f_v)


def emit_bound(mk, blk, layer, b):
    cfg = mk.cfg
    sT, sV, sS = mk.sem("sT"), mk.sem("sV"), mk.sem("sS")
    nq = cfg.KC if layer == 0 else cfg.KCV
    c0, c1 = (0, 1) if layer == 0 else (2, 3)
    vtok = sV.n
    sm = mk.sb["smal"]

    def f_t(t):
        mk.wait(t, "T", sV, vtok)
        t.transpose(mk.ps[5][0:1, 0:nq], mk.sb["nrm"][0:nq, c0:c0 + 1],
                    mk.sb["ident_f"][0:nq, 0:nq])
        mk.sems["sT"].inc(
            t.transpose(mk.ps[4][0:1, 0:nq], mk.sb["nrm"][0:nq, c1:c1 + 1],
                        mk.sb["ident_f"][0:nq, 0:nq]))
    blk.tensor(f_t)
    ttok = sT.n

    def f_v(v):
        mk.wait(v, "V", sT, ttok)
        sV.inc(v.reduce_max(sm[0:1, 34:35], mk.ps[5][0:1, 0:nq], axis=AXX))
        sV.inc(v.reduce_max(sm[0:1, 35:36], mk.ps[4][0:1, 0:nq], axis=AXX))
        mk.wait(v, "V", sV, sV.n)
        sV.inc(v.tensor_tensor(sm[0:1, 36:37], sm[0:1, 34:35],
                               sm[0:1, 35:36], ALU.mult))
        sV.inc(v.memset(mk.sb["nrm"][0:nq, c0:c0 + 1], 0.0))
        sV.inc(v.memset(mk.sb["nrm"][0:nq, c1:c1 + 1], 0.0))
    blk.vector(f_v)
    vtok2 = sV.n

    def f_s(s):
        mk.wait(s, "S", sV, vtok2)
        sS.inc(s.activation(out=sm[0:1, 37:38], in_=sm[0:1, 36:37], func=AF.Ln))
        mk.wait(s, "S", sS, sS.n)
        sS.inc(s.activation(out=sm[0:1, 38:39], in_=sm[0:1, 37:38],
                            func=AF.Exp, scale=0.5))
    blk.scalar(f_s)
    stok = sS.n

    def f_v2(v):
        mk.wait(v, "V", sS, stok)
        sNBp = mk.sem("sNB")
        mk.wait(v, "V", sNBp, sNBp.n)
        sV.inc(v.tensor_scalar_mul(sm[0:1, 39:40], sm[0:1, 38:39], -cfg.scale))
    blk.vector(f_v2)
    vtok3 = sV.n

    def f_st(sc):
        mk.wait(sc, "SC", sV, vtok3)
        mk.sem("sNB").inc(
            sc.dma_start(out=mk.dram["NB"].ap()[layer:layer + 1, b:b + 1],
                         in_=sm[0:1, 39:40]), 16)
    blk.scalar(f_st)


def emit_negB_col(mk, blk, layer, b):
    sV, sT, sS = mk.sem("sV"), mk.sem("sT"), mk.sem("sS")
    sL = mk.sem("sNBL")
    nbst = mk.sems["sNB"].n if "sNB" in mk.sems else 0

    def f_ld(sync):
        if nbst:
            mk.wait(sync, "SY", mk.sems["sNB"], nbst)
        sL.inc(sync.dma_start(
            out=mk.sb["smal"][0:1, 40:41],
            in_=mk.dram["NB"].ap()[layer:layer + 1, b:b + 1]), 16)
    blk.sync(f_ld)
    ltok = sL.n

    def f_v(v):
        mk.wait(v, "V", sL, ltok)
        sV.inc(v.tensor_copy(out=mk.sb["rowsb"][0:1, 0:1],
                             in_=mk.sb["smal"][0:1, 40:41]))
    blk.vector(f_v)
    vtok = sV.n

    def f_t(t):
        mk.wait(t, "T", sV, vtok)
        mk.wait(t, "T", sS, sS.n)
        mk.sem("sT").inc(t.matmul(mk.ps[5][:, 0:1],
                                  mk.sb["ones_row"][0:1, 0:D],
                                  mk.sb["rowsb"][0:1, 0:1],
                                  start=True, stop=True))
    blk.tensor(f_t)
    ttok = mk.sems["sT"].n

    def f_s(s):
        mk.wait(s, "S", sT, ttok)
        mk.sem("sS").inc(s.copy(out=mk.sb["negB"][:, layer:layer + 1],
                                in_=mk.ps[5][:, 0:1]))
    blk.scalar(f_s)
    mk.attn_state["negb_tok"] = mk.sems["sS"].n
    mk.attn_state["rb_evac"] = max(mk.attn_state.get("rb_evac", 0),
                                   mk.sems["sS"].n)
    return mk.sems["sS"].n



def emit_qkv(mk, blk, layer, b, tq, xh_toks, xh_tok):
    cfg = mk.cfg
    sT, sV, sS, sP = mk.sem("sT"), mk.sem("sV"), mk.sem("sS"), mk.sem("sP")
    sb = mk.sb
    ws = WStream(mk, blk)
    stg = mk.stg
    kclist = list(range(cfg.KC))
    rhs = lambda kc: sb["xhat"][:, kc * TILE:(kc + 1) * TILE]

    if layer == 0:
        qw, kw, vw = "q0", "k0", "v0"
        qdst, kdst, vdst = "Q0", "K0", "V0"
        nkv_chunks = cfg.KC
        bq_off = bk_off = bv_off = None
    else:
        qw, kw, vw = "q1", "k1", "v1"
        qdst, kdst, vdst = "QE1", "K1", "V1"
        nkv_chunks = cfg.KCV
        bq_off, bk_off = 0, 2 * cfg.E
        bv_off = 2 * cfg.E + cfg.CKV

    def mk_evac(dst, store_direct, raw_base=16):
        def evac(blk_, mq, psbase, nm, ttok):
            ptok0 = sP.n

            def f_ev(v):
                mk.wait(v, "V", sT, ttok)
                mk.wait(v, "V", sP, ptok0)
                mk.wait(v, "V", sV, sV.n)
                for m in range(nm):
                    h = mq * 4 + m
                    i = v.tensor_copy(out=ar(mk, raw_base + h),
                                      in_=mk.ps[psbase + m][:, :])
                sV.inc(i)
            blk_.vector(f_ev)
            ev_tok = sV.n
            for m in range(nm):
                h = mq * 4 + m
                raw = ar(mk, raw_base + h)
                rtok = emit_rope(mk, blk_, sV, ev_tok, raw, tq,
                                 4 + (h % 2), h % 4, eng="P")
                if store_direct:
                    s = stg.next()

                    def f_ev2(v, s=s, rtok=rtok, h=h):
                        mk.wait(v, "V", sT, rtok)
                        stg.wait_free(v, "V", s)
                        mk.wait(v, "V", sV, sV.n)
                        sV.inc(v.tensor_copy(out=stg.slot_ap(s),
                                             in_=mk.ps[4 + (h % 2)][:, :]))
                    blk_.vector(f_ev2)
                    mk._rope_ps_free[4 + (h % 2)] = (sV, sV.n)
                    dap = mk.dram[dst].ap()[b, h, :, tq * TILE:(tq + 1) * TILE]
                    stg.store(s, dap, sV, sV.n)
                else:
                    def f_ev2(v, rtok=rtok, h=h):
                        mk.wait(v, "V", sT, rtok)
                        mk.wait(v, "V", sV, sV.n)
                        sV.inc(v.tensor_copy(out=ar(mk, 0 + h),
                                             in_=mk.ps[4 + (h % 2)][:, :]))
                    blk_.vector(f_ev2)
                    mk._rope_ps_free[4 + (h % 2)] = (sV, sV.n)
            return (sV, ev_tok)
        return evac

    fm_proj(mk, blk, ws, qw, 0, rhs, sV, xh_tok, kclist,
            list(range(cfg.KC // 4)), mk_evac(qdst, layer == 0),
            bias_row_off=bq_off, psmode="quad", rhs_toks=xh_toks)
    if layer == 1:
        stok = sV.n

        def f_sum(v):
            mk.wait(v, "V", sV, stok)
            for kv in range(cfg.NKV):
                acc = ar(mk, 16 + kv)
                sV.inc(v.tensor_copy(out=acc, in_=ar(mk, cfg.G * kv)))
                for j in range(1, cfg.G):
                    mk.wait(v, "V", sV, sV.n)
                    sV.inc(v.tensor_tensor(acc, acc, ar(mk, cfg.G * kv + j),
                                           ALU.add))
        blk.vector(f_sum)
        qe_tok = sV.n
        for kv in range(cfg.NKV):
            s = stg.next()

            def f_cp(v, kv=kv, s=s):
                stg.wait_free(v, "V", s)
                mk.wait(v, "V", sV, qe_tok)
                sV.inc(v.tensor_copy(out=stg.slot_ap(s), in_=ar(mk, 16 + kv)))
            blk.vector(f_cp)
            dap = mk.dram[qdst].ap()[b, kv, :, tq * TILE:(tq + 1) * TILE]
            stg.store(s, dap, sV, sV.n)
        kraw = 20
    else:
        kraw = 0

    fm_proj(mk, blk, ws, kw, 0, rhs, sV, xh_tok, kclist,
            list(range(nkv_chunks // 4)), mk_evac(kdst, True, raw_base=kraw),
            bias_row_off=bk_off, psmode="quad")
    if layer == 0:
        emit_norms(mk, blk, col=0, src_base=16, nch=cfg.KC)
        emit_norms(mk, blk, col=1, src_base=0, nch=cfg.KC)
    else:
        emit_norms_strided(mk, blk, col=2,
                           idxs=[16 + kv for kv in range(cfg.NKV)])
        emit_norms(mk, blk, col=3, src_base=kraw, nch=nkv_chunks)

    units, tot, _ = mk.wmeta[vw]
    vbar = (sV, sV.n)
    psfree = [vbar, vbar]
    for oq in range(nkv_chunks // 4):
        u = units[oq]
        ld = ws.load(vw, 0, u)
        for ts in range(4):
            pidx = ts % 2
            pf = psfree[pidx]

            def f_t(t, ld=ld, ts=ts, pidx=pidx, pf=pf, u=u):
                mk.wait(t, "T", sV, xh_tok)
                mk.wait(t, "T", ld.sem, ld.tok)
                if pf is not None:
                    mk.wait(t, "T", pf[0], pf[1])
                last = None
                bias = bv_off is not None
                for ki in range(u.nkc):
                    last = t.matmul(
                        mk.ps[pidx][:, 0:u.ncols],
                        sb["xhat"][:, ki * TILE + ts * D:ki * TILE + (ts + 1) * D],
                        ld.buf[:, ki * u.ncols:(ki + 1) * u.ncols],
                        start=(ki == 0), stop=(ki == u.nkc - 1 and not bias))
                if bias:
                    last = t.matmul(
                        mk.ps[pidx][:, 0:u.ncols],
                        sb["ones_row"][0:1, 0:D],
                        sb["brows"][0:1, bv_off + u.col0:bv_off + u.col0 + u.ncols],
                        start=False, stop=True)
                sT.inc(last)
            blk.tensor(f_t)
            ttok = sT.n
            s = stg.next()

            def f_ev(v, ttok=ttok, pidx=pidx, s=s, u=u):
                mk.wait(v, "V", sT, ttok)
                stg.wait_free(v, "V", s)
                sV.inc(v.tensor_copy(out=stg.slot_ap(s)[:, 0:u.ncols],
                                     in_=mk.ps[pidx][:, 0:u.ncols]))
            blk.vector(f_ev)
            psfree[pidx] = (sV, sV.n)
            vt = sV.n
            nslice = u.ncols // D

            def f_st(sc, s=s, ts=ts, oq=oq, vt=vt, nslice=nslice):
                mk.wait(sc, "SC", sV, vt)
                for j in range(nslice):
                    h = oq * 4 + j
                    dap = mk.dram[vdst].ap()[
                        b, h, tq * TILE + ts * D:tq * TILE + (ts + 1) * D, :]
                    stg.sems[s].inc(
                        sc.dma_start(out=dap,
                                     in_=stg.slot_ap(s)[:, j * D:(j + 1) * D]), 16)
            blk.scalar(f_st)
            stg.uses[s] += nslice - 1
            stg.uses[s] += 1
        ws.mark_consumed(ld.slot, sT, sT.n)


def emit_norms_strided(mk, blk, col, idxs):
    cfg = mk.cfg
    sT, sV, sP = mk.sem("sT"), mk.sem("sV"), mk.sem("sP")
    nE = cfg.KCV
    Emat = mk.sb["E4"]
    ptok0 = sP.n
    tbase = sT.n
    vtok0 = sV.n
    stok0 = mk.sems["sS"].n

    def f_p(p):
        mk.wait(p, "P", sV, vtok0)
        mk.wait(p, "P", mk.sems["sS"], stok0)
        for i, idx in enumerate(idxs):
            if i >= 2:
                mk.wait(p, "P", sT, tbase + (i - 1), future=True)
            src = ar(mk, idx)
            sP.inc(p.tensor_tensor(ar(mk, 62 + (i % 2)), src, src, ALU.mult))
    blk.gpsimd(f_p)

    vcur0 = sV.n

    def f_t(t):
        mk.wait(t, "T", sV, vcur0)
        for i in range(len(idxs)):
            mk.wait(t, "T", sP, ptok0 + i + 1)
            sT.inc(t.matmul(mk.ps[2][0:nE, :],
                            Emat[:, i * nE:(i + 1) * nE],
                            ar(mk, 62 + (i % 2)),
                            start=(i == 0), stop=(i == len(idxs) - 1)))
    blk.tensor(f_t)
    ttok = sT.n

    def f_v(v):
        mk.wait(v, "V", sT, ttok)
        sV.inc(v.reduce_max(mk.sb["smal"][0:nE, 32:33], mk.ps[2][0:nE, :],
                            axis=AXX))
        mk.wait(v, "V", sV, sV.n)
        sV.inc(v.tensor_tensor(mk.sb["nrm"][0:nE, col:col + 1],
                               mk.sb["nrm"][0:nE, col:col + 1],
                               mk.sb["smal"][0:nE, 32:33], ALU.max))
    blk.vector(f_v)



def emit_attn2(mk, blk, layer, b, h0):
    cfg = mk.cfg
    sT, sV, sS, sP = mk.sem("sT"), mk.sem("sV"), mk.sem("sS"), mk.sem("sP")
    sb = mk.sb
    qsrc = "Q0" if layer == 0 else "QE1"
    ksrc = "K0" if layer == 0 else "K1"
    vsrc = "V0" if layer == 0 else "V1"
    odst = "O0" if layer == 0 else "O1"
    st = mk.attn_state
    stride = 3 * cfg.TQ
    bufs = []
    for par in (0, 1):
        qb = ar(mk, 40 + par * stride, cfg.TQ)
        kb = ar(mk, 40 + par * stride + cfg.TQ, cfg.TQ)
        vb = ar(mk, 40 + par * stride + 2 * cfg.TQ, cfg.TQ)
        bufs.append((qb, kb, vb))
    sQL = mk.sem("sQL0")
    sOst = [mk.sem(f"sOst{i}") for i in range(8)]
    prev_cons = st.get("cons")
    negB = sb["negB"][:, layer:layer + 1]
    negb_tok = st.get("negb_tok", 0)
    daccB = sb["fstage"]

    def f_ld(sync):
        if prev_cons is not None:
            mk.wait(sync, "SY", prev_cons[0], prev_cons[1], future=True)
        for x, h in ((0, h0), (1, h0 + 1)):
            qb, kb, vb = bufs[x]
            sQL.inc(sync.dma_start(out=qb, in_=mk.dram[qsrc].ap()[b, h]), 16)
            sQL.inc(sync.dma_start(out=kb, in_=mk.dram[ksrc].ap()[b, h]), 16)
            sQL.inc(sync.dma_start(
                out=vb.rearrange("p (k d) -> p k d", d=D),
                in_=mk.dram[vsrc].ap()[b, h].rearrange("(k p) d -> p k d", p=D)), 16)
    blk.sync(f_ld)
    ldtok = sQL.n

    groups = list(range(cfg.TQ))
    PT0 = {0: 0, 1: 10}
    psO_ = {0: mk.ps[3], 1: mk.ps[4]}

    def ptpair(x, k):
        return ar(mk, PT0[x] + 2 * (k % 2), 2)

    def ptc(x, c):
        return ar(mk, PT0[x] + (c % 4))

    nT, nS, nV = sT.n, sS.n, sV.n
    smm, expp, pv, dnm = {}, {}, {}, {}
    maskt, accB = {}, {}
    dmmA, dmmB, lnd, expd, rbmm, rbcopy, fin = {}, {}, {}, {}, {}, {}, {}
    for g in groups:
        nk = 4 * (g + 1)
        npair = nk // 2
        for x in (0, 1):
            nT += 2; smm[(x, g, 0)] = nT
        if g > 0:
            for x in (0, 1):
                nT += 1; rbmm[(x, g - 1)] = nT
        for k in range(npair):
            for x in (0, 1):
                if k + 1 < npair:
                    nT += 2; smm[(x, g, k + 1)] = nT
                for c in (2 * k, 2 * k + 1):
                    nT += 1; pv[(x, g, c)] = nT
                    if x == 0:
                        nT += 1; dnm[(g, c)] = nT
        dmmA[g] = dnm[(g, nk - 1)]
        nT += 1; dmmB[g] = nT
        for k in range(npair):
            for x in (0, 1):
                nS += 1; expp[(x, g, k)] = nS
        for x in (0, 1):
            nS += 1; lnd[(x, g)] = nS
            nS += 1; expd[(x, g)] = nS
        for k in range(npair):
            for c in (2 * k, 2 * k + 1):
                for x in (0, 1):
                    if c >= nk - 4:
                        nV += 1; maskt[(x, g, c)] = nV
                    if x == 1:
                        nV += 1; accB[(g, c)] = nV
        nV += 1
        nV -= 1
        for x in (0, 1):
            nV += 1; rbcopy[(x, g)] = nV
            nV += 1; fin[(x, g)] = nV
    gl = groups[-1]
    for x in (0, 1):
        nT += 1; rbmm[(x, gl)] = nT

    pv_prev_exp = st.get("exp_last", 0)
    pv_prev_maskA = st.get("maskA_last", 0)
    pv_prev_maskB = st.get("maskB_last", 0)
    prev_accB = st.get("accB_last", 0)
    prev_dmmB = st.get("dmmB_last", 0)
    prev_lndB = st.get("lndB_last", 0)
    prev_rbcopyB = st.get("rbcopyB_last", 0)
    prev_rbmmB = st.get("rbmmB_last", 0)
    prev_finA = st.get("finA_last")
    prev_finB = st.get("finB_last")
    ost_uses = [st.get(f"ost_uses{i}", 0) for i in range(8)]

    rrows = [sb["stats"][0:1, TILE:2 * TILE], sb["stats"][0:1, 2 * TILE:3 * TILE]]
    rbfs = [sb["rowsb"][0:1, 0:TILE], sb["rowsb"][0:1, TILE:2 * TILE]]

    def f_t(t):
        for g in groups:
            nk = 4 * (g + 1)
            npair = nk // 2
            qs = slice(g * TILE, (g + 1) * TILE)

            def wait_pair(x, k):
                if k >= 1:
                    mk.wait(t, "T", sS, expp[(x, g, k - 1)], future=True)
                elif g > 0:
                    mk.wait(t, "T", sS, expp[(x, g - 1, 2 * g - 1)], future=True)
                elif pv_prev_exp:
                    mk.wait(t, "T", sS, pv_prev_exp, future=True)

            mk.wait(t, "T", sQL, ldtok)
            for x in (0, 1):
                qb, kb, vb = bufs[x]
                wait_pair(x, 0)
                pair = mk.psp[x]
                t.matmul(pair[:, 0:TILE], kb[:, 0:D], qb[:, qs],
                         start=True, stop=True)
                sT.inc(t.matmul(pair[:, TILE:2 * TILE], kb[:, D:2 * D],
                                qb[:, qs], start=True, stop=True), 2)
                assert sT.n == smm[(x, g, 0)]
            if g > 0:
                for x in (0, 1):
                    mk.wait(t, "T", sS, expd[(x, g - 1)], future=True)
                    if x == 1:
                        mk.wait(t, "T", sV, rbcopy[(0, g - 1)], future=True)
                    elif g > 1:
                        mk.wait(t, "T", sV, rbcopy[(1, g - 2)], future=True)
                    elif prev_rbcopyB:
                        mk.wait(t, "T", sV, prev_rbcopyB, future=True)
                    sT.inc(t.matmul(mk.ps[5][:, :], sb["ones_row"][0:1, 0:D],
                                    rbfs[x], start=True, stop=True))
                    assert sT.n == rbmm[(x, g - 1)]
            for k in range(npair):
                for x in (0, 1):
                    qb, kb, vb = bufs[x]
                    if k + 1 < npair:
                        wait_pair(x, k + 1)
                        pair = mk.psp[x]
                        c2 = 2 * (k + 1)
                        t.matmul(pair[:, 0:TILE], kb[:, c2 * D:(c2 + 1) * D],
                                 qb[:, qs], start=True, stop=True)
                        sT.inc(t.matmul(pair[:, TILE:2 * TILE],
                                        kb[:, (c2 + 1) * D:(c2 + 2) * D],
                                        qb[:, qs], start=True, stop=True), 2)
                        assert sT.n == smm[(x, g, k + 1)]
                    for c in (2 * k, 2 * k + 1):
                        if c >= nk - 4:
                            mk.wait(t, "T", sV, maskt[(x, g, c)], future=True)
                        else:
                            mk.wait(t, "T", sS, expp[(x, g, k)], future=True)
                        if c == 0:
                            fp = (fin[(x, g - 1)] if g > 0 else None)
                            if fp is not None:
                                mk.wait(t, "T", sV, fp, future=True)
                            elif x == 0 and prev_finA is not None:
                                mk.wait(t, "T", prev_finA[0], prev_finA[1],
                                        future=True)
                            elif x == 1 and prev_finB is not None:
                                mk.wait(t, "T", prev_finB[0], prev_finB[1],
                                        future=True)
                            if x == 0:
                                if g > 0:
                                    mk.wait(t, "T", sS, lnd[(1, g - 1)],
                                            future=True)
                                elif prev_lndB:
                                    mk.wait(t, "T", sS, prev_lndB, future=True)
                        sT.inc(t.matmul(psO_[x][:, :], vb[:, c * D:(c + 1) * D],
                                        ptc(x, c), start=(c == 0),
                                        stop=(c == nk - 1)))
                        assert sT.n == pv[(x, g, c)]
                        if x == 0:
                            sT.inc(t.matmul(mk.ps[2][0:1, :], sb["ones_col"][:],
                                            ptc(0, c), start=(c == 0),
                                            stop=(c == nk - 1)))
                            assert sT.n == dnm[(g, c)]
            mk.wait(t, "T", sV, accB[(g, nk - 1)], future=True)
            mk.wait(t, "T", sS, lnd[(0, g)], future=True)
            sT.inc(t.matmul(mk.ps[2][0:1, :], sb["onesf"][:], daccB[:],
                            start=True, stop=True))
            assert sT.n == dmmB[g]
        for x in (0, 1):
            mk.wait(t, "T", sS, expd[(x, gl)], future=True)
            if x == 1:
                mk.wait(t, "T", sV, rbcopy[(0, gl)], future=True)
            elif gl > 0:
                mk.wait(t, "T", sV, rbcopy[(1, gl - 1)], future=True)
            elif prev_rbcopyB:
                mk.wait(t, "T", sV, prev_rbcopyB, future=True)
            sT.inc(t.matmul(mk.ps[5][:, :], sb["ones_row"][0:1, 0:D],
                            rbfs[x], start=True, stop=True))
            assert sT.n == rbmm[(x, gl)]
    blk.tensor(f_t)
    assert sT.n == nT

    def f_s(s):
        first = True
        for g in groups:
            nk = 4 * (g + 1)
            npair = nk // 2
            for k in range(npair):
                for x in (0, 1):
                    mk.wait(s, "S", sT, smm[(x, g, k)])
                    if first and negb_tok:
                        mk.wait(s, "S", sS, negb_tok)
                        first = False
                    kk = k - 2
                    if kk >= 0:
                        if x == 0:
                            mk.wait(s, "S", sT, dnm[(g, 2 * kk + 1)])
                        else:
                            mk.wait(s, "S", sV, accB[(g, 2 * kk + 1)],
                                    future=True)
                        if (x, g, 2 * kk + 1) in maskt:
                            mk.wait(s, "S", sV, maskt[(x, g, 2 * kk + 1)],
                                    future=True)
                    elif g > 0:
                        cl = 4 * g - 1
                        if x == 0:
                            mk.wait(s, "S", sT, dnm[(g - 1, cl)])
                        else:
                            mk.wait(s, "S", sV, accB[(g - 1, cl)], future=True)
                        mk.wait(s, "S", sV, maskt[(x, g - 1, cl)], future=True)
                    else:
                        if x == 0 and pv_prev_maskA:
                            mk.wait(s, "S", sV, pv_prev_maskA)
                        if x == 1 and pv_prev_maskB:
                            mk.wait(s, "S", sV, pv_prev_maskB)
                        if x == 1 and prev_accB:
                            mk.wait(s, "S", sV, prev_accB)
                    sS.inc(s.activation(out=ptpair(x, k), in_=mk.psp[x][:, :],
                                        func=AF.Exp, bias=negB,
                                        scale=cfg.scale))
                    assert sS.n == expp[(x, g, k)]
            mk.wait(s, "S", sT, dmmA[g])
            sS.inc(s.activation(out=rrows[0], in_=mk.ps[2][0:1, :], func=AF.Ln))
            assert sS.n == lnd[(0, g)]
            mk.wait(s, "S", sS, sS.n)
            mk.wait(s, "S", sT, rbmm[(0, g - 1)] if g > 0 else
                    st.get("rbmmA_last", 0))
            sS.inc(s.activation(out=rbfs[0], in_=rrows[0], func=AF.Exp,
                                scale=-1.0))
            assert sS.n == expd[(0, g)]
            mk.wait(s, "S", sT, dmmB[g])
            sS.inc(s.activation(out=rrows[1], in_=mk.ps[2][0:1, :], func=AF.Ln))
            assert sS.n == lnd[(1, g)]
            mk.wait(s, "S", sS, sS.n)
            mk.wait(s, "S", sT, rbmm[(1, g - 1)] if g > 0 else prev_rbmmB)
            sS.inc(s.activation(out=rbfs[1], in_=rrows[1], func=AF.Exp,
                                scale=-1.0))
            assert sS.n == expd[(1, g)]
    blk.scalar(f_s)
    assert sS.n == nS

    def f_v(v):
        for g in groups:
            nk = 4 * (g + 1)
            npair = nk // 2
            for k in range(npair):
                for c in (2 * k, 2 * k + 1):
                    for x in (0, 1):
                        pt = ptc(x, c)
                        mk.wait(v, "V", sS, expp[(x, g, k)])
                        if c >= nk - 4:
                            j = c - (nk - 4)
                            mk.wait(v, "V", sV, sV.n)
                            sV.inc(v.tensor_tensor(
                                pt, pt, sb["masks"][:, j * TILE:(j + 1) * TILE],
                                ALU.mult))
                            assert sV.n == maskt[(x, g, c)]
                        if x == 1:
                            mk.wait(v, "V", sV, sV.n)
                            if c == 0:
                                mk.wait(v, "V", sT,
                                        dmmB[g - 1] if g > 0 else prev_dmmB)
                                sV.inc(v.tensor_copy(out=daccB[:], in_=pt))
                            else:
                                sV.inc(v.tensor_tensor(daccB[:], daccB[:], pt,
                                                       ALU.add))
                            assert sV.n == accB[(g, c)]
            for x in (0, 1):
                osl = 4 * x + g
                mk.wait(v, "V", sT, rbmm[(x, g)])
                mk.wait(v, "V", sV, sV.n)
                sV.inc(v.tensor_copy(out=ar(mk, 14 + x), in_=mk.ps[5][:, :]))
                assert sV.n == rbcopy[(x, g)]
                if ost_uses[osl] > 0:
                    mk.wait(v, "V", sOst[osl], 16 * ost_uses[osl], future=True)
                mk.wait(v, "V", sV, sV.n)
                sV.inc(v.tensor_tensor(ar(mk, 16 + osl), psO_[x][:, :],
                                       ar(mk, 14 + x), ALU.mult))
                assert sV.n == fin[(x, g)]
                ost_uses[osl] += 1
    blk.vector(f_v)
    assert sV.n == nV

    def f_st(sc):
        for g in groups:
            for x in (0, 1):
                osl = 4 * x + g
                mk.wait(sc, "SC", sV, fin[(x, g)])
                dap = mk.dram[odst].ap()[b, h0 + x, :,
                                         g * TILE:(g + 1) * TILE]
                sOst[osl].inc(sc.dma_start(out=dap, in_=ar(mk, 16 + osl)), 16)
    blk.scalar(f_st)

    st["cons"] = (sT, sT.n)
    nkl = 4 * cfg.TQ
    st["exp_last"] = expp[(1, gl, nkl // 2 - 1)]
    st["maskA_last"] = maskt[(0, gl, nkl - 1)]
    st["maskB_last"] = maskt[(1, gl, nkl - 1)]
    st["accB_last"] = accB[(gl, nkl - 1)]
    st["dmmB_last"] = dmmB[gl]
    st["lndB_last"] = lnd[(1, gl)]
    st["rbcopyB_last"] = rbcopy[(1, gl)]
    st["rbmmA_last"] = rbmm[(0, gl)]
    st["rbmmB_last"] = rbmm[(1, gl)]
    st["finA_last"] = (sV, fin[(0, gl)])
    st["finB_last"] = (sV, fin[(1, gl)])
    for i in range(8):
        st[f"ost_uses{i}"] = ost_uses[i]



def emit_p3_tile(mk, blk, b, tq):
    cfg = mk.cfg
    sT, sV, sS = mk.sem("sT"), mk.sem("sV"), mk.sem("sS")
    sb = mk.sb
    ws = WStream(mk, blk)
    sXL = mk.sem("sXL")
    sOL = mk.sem("sOL")
    obf = ar(mk, 0, cfg.KC)

    prevu = getattr(mk, "p3_prev_user", None)
    if prevu is None:
        prevu = [(sV, sV.n), (sT, sT.n), (sS, sS.n)]

    def f_ldo(sc):
        for sm_, tk_ in prevu:
            mk.wait(sc, "SC", sm_, tk_)
        apo = mk.dram["O0"].ap()[b, :, :, tq * TILE:(tq + 1) * TILE]
        sOL.inc(sc.dma_start(
            out=obf.rearrange("p (k t) -> p k t", k=cfg.KC),
            in_=apo.rearrange("k p t -> p k t")), 16)
    blk.scalar(f_ldo)
    otok = sOL.n

    vtok_x = sV.n

    def f_ldx(sync):
        mk.wait(sync, "SY", sV, vtok_x)
        for nm_ in ("sX2st",):
            if nm_ in mk.sems:
                mk.wait(sync, "SY", mk.sems[nm_], mk.sems[nm_].n)
        sXL.inc(sync.dma_start(
            out=sb["Xres"][:].rearrange("p (k t) -> p k t", k=cfg.KC),
            in_=mk.dram["Xin"].ap()[b, tq]), 16)
    blk.sync(f_ldx)
    xltok = sXL.n

    def rhs(kc):
        return obf[:, kc * TILE:(kc + 1) * TILE]

    def evac(blk_, mq, psbase, nm, ttok_):
        def f_v(v):
            mk.wait(v, "V", sT, ttok_)
            mk.wait(v, "V", sXL, xltok)
            for m in range(nm):
                kc = mq * 4 + m
                x = sb["Xres"][:, kc * TILE:(kc + 1) * TILE]
                i = v.tensor_tensor(x, mk.ps[psbase + m][:, :], x, ALU.add)
            sV.inc(i)
        blk_.vector(f_v)
        return (sV, sV.n)

    fm_proj(mk, blk, ws, "o0", 0, rhs, sOL, otok,
            list(range(cfg.KC)), list(range(cfg.KC // 4)), evac)
    x1_tok = sV.n

    def f_cast(v):
        mk.wait(v, "V", sV, x1_tok)
        mk.wait(v, "V", sT, sT.n)
        for kc in range(cfg.KC):
            i = v.tensor_copy(out=obf[:, kc * TILE:(kc + 1) * TILE],
                              in_=sb["Xres"][:, kc * TILE:(kc + 1) * TILE])
        sV.inc(i)
    blk.vector(f_cast)

    xh_toks, xh_tok = emit_ln(mk, blk, b, tq, None, xbf_off=0, sq_off=16,
                              xbf_ready=(sV, sV.n))

    units, tot, _ = mk.wmeta["gu0"]
    ngq = cfg.DFF // TILE
    prev_gate = None
    prev_up = None
    for mq in range(ngq):
        ldg = ws.load("gu0", 0, units[mq])

        def f_tg(t, ldg=ldg, pf=prev_gate, mq=mq):
            if mq > 0:
                mk.wait(t, "T", sV, xh_tok)
            mk.wait(t, "T", ldg.sem, ldg.tok)
            if pf is not None:
                mk.wait(t, "T", pf[0], pf[1])
            last = None
            for ki in range(cfg.KC):
                if mq == 0 and ki % 4 == 0:
                    mk.wait(t, "T", sV, xh_toks[min(ki + 3, cfg.KC - 1)])
                for m in range(4):
                    last = t.matmul(
                        mk.ps[m][:, :],
                        ldg.buf[:, ki * TILE + m * D:ki * TILE + (m + 1) * D],
                        sb["xhat"][:, ki * TILE:(ki + 1) * TILE],
                        start=(ki == 0), stop=(ki == cfg.KC - 1))
            sT.inc(last)
        blk.tensor(f_tg)
        gate_tok = sT.n
        ws.mark_consumed(ldg.slot, sT, gate_tok)
        ldu = ws.load("gu0", 0, units[ngq + mq])

        def f_tu(t, ldu=ldu, pf=prev_up):
            mk.wait(t, "T", ldu.sem, ldu.tok)
            if pf is not None:
                mk.wait(t, "T", pf[0], pf[1])
            last = None
            for ki in range(cfg.KC):
                for m in range(4):
                    last = t.matmul(
                        mk.ps[4 + m][:, :],
                        ldu.buf[:, ki * TILE + m * D:ki * TILE + (m + 1) * D],
                        sb["xhat"][:, ki * TILE:(ki + 1) * TILE],
                        start=(ki == 0), stop=(ki == cfg.KC - 1))
            sT.inc(last)
        blk.tensor(f_tu)
        up_tok = sT.n
        ws.mark_consumed(ldu.slot, sT, up_tok)

        def f_si(s, gate_tok=gate_tok, pu=prev_up):
            mk.wait(s, "S", sT, gate_tok)
            if pu is not None:
                mk.wait(s, "S", pu[0], pu[1])
            for m in range(4):
                i = s.activation(out=ar(mk, 64 + m), in_=mk.ps[m][:, :],
                                 func=AF.Silu)
            sS.inc(i)
        blk.scalar(f_si)
        si_tok = sS.n
        prev_gate = (sS, si_tok)

        def f_h(v, si_tok=si_tok, up_tok=up_tok, mq=mq):
            mk.wait(v, "V", sS, si_tok)
            mk.wait(v, "V", sT, up_tok)
            for m in range(4):
                fc = mq * 4 + m
                i = v.tensor_tensor(ar(mk, fc), ar(mk, 64 + m),
                                    mk.ps[4 + m][:, :], ALU.mult)
            sV.inc(i)
        blk.vector(f_h)
        prev_up = (sV, sV.n)
    h_tok = sV.n

    units_d, _, _ = mk.wmeta["dn0"]
    nu = (cfg.KFF + 15) // 16
    prev_free = [None, None]
    nmq_d = cfg.KC // 4
    for mq in range(nmq_d):
        psbase = 4 * (mq % 2)
        pf = prev_free[mq % 2]
        for kq in range(nu):
            u = units_d[mq * nu + kq]
            ld = ws.load("dn0", 0, u)

            def f_t(t, ld=ld, u=u, kq=kq, psbase=psbase, pf=pf):
                mk.wait(t, "T", sV, h_tok)
                mk.wait(t, "T", ld.sem, ld.tok)
                if kq == 0 and pf is not None:
                    mk.wait(t, "T", pf[0], pf[1])
                last = None
                for ki in range(u.nkc):
                    fc = u.kc0 + ki
                    for m in range(4):
                        last = t.matmul(
                            mk.ps[psbase + m][:, :],
                            ld.buf[:, ki * TILE + m * D:ki * TILE + (m + 1) * D],
                            ar(mk, fc),
                            start=(kq == 0 and ki == 0),
                            stop=(kq == nu - 1 and ki == u.nkc - 1))
                sT.inc(last)
            blk.tensor(f_t)
            ws.mark_consumed(ld.slot, sT, sT.n)
            if mq == nmq_d - 1 and kq == 0:
                mk.p3_prev_user = [(sT, sT.n)]
        ttok_ = sT.n

        def f_v(v, mq=mq, psbase=psbase, ttok_=ttok_):
            mk.wait(v, "V", sT, ttok_)
            for m in range(4):
                kc = mq * 4 + m
                x = sb["Xres"][:, kc * TILE:(kc + 1) * TILE]
                i = v.tensor_tensor(x, mk.ps[psbase + m][:, :], x, ALU.add)
            sV.inc(i)
        blk.vector(f_v)
        prev_free[mq % 2] = (sV, sV.n)
    x2_tok = sV.n

    def f_st(sc):
        mk.wait(sc, "SC", sV, x2_tok)
        mk.sem("sX2st").inc(sc.dma_start(
            out=mk.dram["X2"].ap()[b, tq],
            in_=sb["Xres"][:].rearrange("p (k t) -> p k t", k=cfg.KC)), 16)
    blk.scalar(f_st)



def emit_p6_tile(mk, blk, b, tq):
    cfg = mk.cfg
    sT, sV, sS = mk.sem("sT"), mk.sem("sV"), mk.sem("sS")
    sb = mk.sb
    sm = sb["smal"]
    ws = WStream(mk, blk)
    sXL = mk.sem("sXL")
    sOL = mk.sem("sOL")
    obf = ar(mk, 0, cfg.KCV)

    prevu = getattr(mk, "p6_prev_user", None)
    if prevu is None:
        prevu = [(sV, sV.n), (sT, sT.n), (sS, sS.n)]

    def f_ldo(sc):
        for sm_, tk_ in prevu:
            mk.wait(sc, "SC", sm_, tk_)
        apo = mk.dram["O1"].ap()[b, :, :, tq * TILE:(tq + 1) * TILE]
        sOL.inc(sc.dma_start(
            out=obf.rearrange("p (k t) -> p k t", k=cfg.KCV),
            in_=apo.rearrange("k p t -> p k t")), 16)
    blk.scalar(f_ldo)
    otok = sOL.n

    vtok_x = sV.n

    def f_ldx(sync):
        mk.wait(sync, "SY", sV, vtok_x)
        for nm_ in ("sOutSt", "sX3st"):
            if nm_ in mk.sems:
                mk.wait(sync, "SY", mk.sems[nm_], mk.sems[nm_].n)
        sXL.inc(sync.dma_start(
            out=sb["Xres"][:].rearrange("p (k t) -> p k t", k=cfg.KC),
            in_=mk.dram["X2"].ap()[b, tq]), 16)
    blk.sync(f_ldx)
    xltok = sXL.n

    def rhs(kc):
        return obf[:, kc * TILE:(kc + 1) * TILE]

    def evac(blk_, mq, psbase, nm, ttok_):
        def f_v(v):
            mk.wait(v, "V", sT, ttok_)
            mk.wait(v, "V", sXL, xltok)
            for m in range(nm):
                kc = mq * 4 + m
                x = sb["Xres"][:, kc * TILE:(kc + 1) * TILE]
                i = v.tensor_tensor(x, mk.ps[psbase + m][:, :], x, ALU.add)
            sV.inc(i)
        blk_.vector(f_v)
        return (sV, sV.n)

    fm_proj(mk, blk, ws, "o1", 0, rhs, sOL, otok,
            list(range(cfg.KCV)), list(range(cfg.KC // 4)), evac,
            bias_row_off=cfg.E)
    x3_tok = sV.n

    def f_stx3(sc):
        mk.wait(sc, "SC", sV, x3_tok)
        mk.sem("sX3st").inc(sc.dma_start(
            out=mk.dram["X3"].ap()[b, tq],
            in_=sb["Xres"][:].rearrange("p (k t) -> p k t", k=cfg.KC)), 16)
    blk.scalar(f_stx3)
    x3st = mk.sems["sX3st"].n

    def f_cast(v):
        mk.wait(v, "V", sV, x3_tok)
        mk.wait(v, "V", sT, sT.n)
        for kc in range(cfg.KC):
            i = v.tensor_copy(out=ar(mk, kc),
                              in_=sb["Xres"][:, kc * TILE:(kc + 1) * TILE])
        sV.inc(i)
    blk.vector(f_cast)

    xh_toks, xh_tok = emit_ln(mk, blk, b, tq, None, xbf_off=0, sq_off=16,
                              xbf_ready=(sV, sV.n))

    def f_tg(t):
        mk.wait(t, "T", sXL, xltok)
        mk.wait(t, "T", sV, xh_tok)
        for kc in range(cfg.KC):
            i = t.matmul(mk.ps[6][0:cfg.NE, :],
                         sb["wg"][:, kc * cfg.NE:(kc + 1) * cfg.NE],
                         sb["Xres"][:, kc * TILE:(kc + 1) * TILE],
                         start=(kc == 0), stop=False)
        sT.inc(i)
    blk.tensor(f_tg)

    def f_vn(v):
        mk.wait(v, "V", sV, xh_tok)
        sV.inc(v.tensor_scalar_mul(sm[0:1, 40:40 + cfg.NE], sb["csum"][:], -1.0))
        sV.inc(v.memset(sm[0:1, 48:48 + cfg.NE], 1.0))
    blk.vector(f_vn)
    vtok_n = sV.n

    def f_tg2(t):
        mk.wait(t, "T", sV, vtok_n)
        sT.inc(t.matmul(mk.ps[6][0:cfg.NE, :], sm[0:1, 40:40 + cfg.NE],
                        sb["stats"][0:1, 0:TILE], start=False, stop=True))
        sT.inc(t.matmul(mk.ps[5][0:cfg.NE, :], sm[0:1, 48:48 + cfg.NE],
                        sb["stats"][0:1, 3 * TILE:4 * TILE],
                        start=True, stop=True))
    blk.tensor(f_tg2)
    tg_tok = sT.n

    def f_vg(v):
        mk.wait(v, "V", sT, tg_tok)
        sV.inc(v.tensor_copy(out=sb["g8"][:, :], in_=mk.ps[5][0:cfg.NE, :]))
        mk.wait(v, "V", sV, sV.n)
        sV.inc(v.tensor_tensor(sb["g8s"][:, :], mk.ps[6][0:cfg.NE, :],
                               sb["g8"][:, :], ALU.mult))
    blk.vector(f_vg)
    g_tok = sV.n

    sm_st = lambda ts: sm[:, ts * 8:(ts + 1) * 8]
    sm_s2 = lambda ts: sm[:, 32 + ts * 8:40 + ts * 8]
    sm_m1 = lambda ts: sm[:, 64 + ts:65 + ts]
    sm_w = lambda ts: sm[:, 96 + ts * 8:104 + ts * 8]
    mx = sm[:, 68:76]
    din = sm[:, 88:92]
    dex = sm[:, 92:96]
    for ts in range(4):
        def f_tt(t, ts=ts):
            mk.wait(t, "T", sV, g_tok if ts == 0 else sV.n)
            sT.inc(t.transpose(mk.ps[5][0:D, 0:cfg.NE],
                               sb["g8s"][:, ts * D:(ts + 1) * D],
                               sb["ident_f"][0:cfg.NE, 0:cfg.NE]))
        blk.tensor(f_tt)
        tt_tok = sT.n

        def f_vt(v, tt_tok=tt_tok, ts=ts):
            mk.wait(v, "V", sT, tt_tok)
            mk.wait(v, "V", sV, sV.n)
            sV.inc(v.tensor_copy(out=sm_st(ts), in_=mk.ps[5][0:D, 0:cfg.NE]))
            mk.wait(v, "V", sV, sV.n)
            sV.inc(v.max(mx, sm_st(ts)))
            mk.wait(v, "V", sV, sV.n)
            sV.inc(v.tensor_copy(out=sm_m1(ts), in_=mx[:, 0:1]))
            sV.inc(v.memset(mx[:, 2:8], -1e30))
            mk.wait(v, "V", sV, sV.n)
            sV.inc(v.match_replace(sm_s2(ts), mx, sm_st(ts), -1e30))
            sV.inc(v.tensor_tensor(din[:, ts:ts + 1], mx[:, 1:2], mx[:, 0:1],
                                   ALU.subtract))
        blk.vector(f_vt)
    v1 = sV.n

    def f_se(s, v1=v1):
        mk.wait(s, "S", sV, v1)
        sS.inc(s.activation(out=dex, in_=din, func=AF.Exp))
    blk.scalar(f_se)
    s1 = sS.n

    for ts in range(4):
        def f_vw(v, s1=s1, ts=ts):
            mk.wait(v, "V", sS, s1)
            e1c = dex[:, ts:ts + 1]
            c2 = sm[:, 76:77]
            p1 = sm[:, 77:78]
            p2 = sm[:, 78:79]
            pd = sm[:, 79:80]
            dm = sm[:, 80:88]
            mk.wait(v, "V", sV, sV.n)
            sV.inc(v.tensor_scalar_add(c2, e1c, 1.0))
            mk.wait(v, "V", sV, sV.n)
            sV.inc(v.reciprocal(p1, c2))
            mk.wait(v, "V", sV, sV.n)
            sV.inc(v.tensor_tensor(p2, e1c, p1, ALU.mult))
            mk.wait(v, "V", sV, sV.n)
            sV.inc(v.tensor_tensor(pd, p1, p2, ALU.subtract))
            sV.inc(v.tensor_tensor(dm, sm_st(ts), sm_s2(ts), ALU.not_equal))
            sV.inc(v.tensor_single_scalar(sm_w(ts), sm_st(ts), sm_m1(ts),
                                          ALU.is_equal))
            mk.wait(v, "V", sV, sV.n)
            sV.inc(v.tensor_scalar(sm_w(ts), sm_w(ts), pd, p2,
                                   ALU.mult, ALU.add))
            mk.wait(v, "V", sV, sV.n)
            sV.inc(v.tensor_tensor(sm_w(ts), sm_w(ts), dm, ALU.mult))
        blk.vector(f_vw)
    w_tok = sV.n

    for e in range(cfg.NE):
        vcur_b = sV.n
        scur_b = sS.n

        def f_tb(t, e=e, w_tok=w_tok, vcur_b=vcur_b):
            mk.wait(t, "T", sV, max(w_tok, vcur_b))
            for ts in range(4):
                i = t.transpose(mk.ps[5][0:1, ts * D:(ts + 1) * D],
                                sm[:, 96 + ts * 8 + e:97 + ts * 8 + e],
                                sb["ident_f"][:, :])
            sT.inc(i)
        blk.tensor(f_tb)
        tb = sT.n

        def f_sr(v, tb=tb):
            mk.wait(v, "V", sT, tb)
            mk.wait(v, "V", sV, sV.n)
            sV.inc(v.tensor_copy(out=sb["rowsb"][0:1, 0:TILE],
                                 in_=mk.ps[5][0:1, 0:TILE]))
        blk.vector(f_sr)
        srt = sV.n

        def f_tb2(t, srt=srt):
            mk.wait(t, "T", sV, srt)
            sT.inc(t.matmul(mk.ps[5][:, :], sb["ones_row"][0:1, 0:D],
                            sb["rowsb"][0:1, 0:TILE],
                            start=True, stop=True))
        blk.tensor(f_tb2)
        tb2 = sT.n

        def f_sb(v, tb2=tb2, e=e):
            mk.wait(v, "V", sT, tb2)
            mk.wait(v, "V", sV, sV.n)
            sV.inc(v.tensor_copy(out=ar(mk, 52 + e), in_=mk.ps[5][:, :]))
        blk.vector(f_sb)
    wbe8_tok = sV.n

    units1, _, _ = mk.wmeta["e1"]
    units3, _, _ = mk.wmeta["e3"]
    nmq1 = len(units1)
    for half in range(2):
        prev_g = (sV, sV.n)
        prev_u = (sV, sV.n)
        for ei in range(4):
            e = half * 4 + ei
            for mq in range(nmq1):
                ug = units1[mq]
                nm = (ug.ncols + D - 1) // D
                ld_g = ws.load("e1", e, ug)

                def f_tg_(t, ld=ld_g, u=ug, nm=nm, pf=prev_g, e=e, mq=mq):
                    first = e == 0 and mq == 0 and half == 0
                    if not first:
                        mk.wait(t, "T", sV, xh_tok)
                    mk.wait(t, "T", ld.sem, ld.tok)
                    if pf is not None:
                        mk.wait(t, "T", pf[0], pf[1])
                    last = None
                    for ki in range(cfg.KC):
                        if first and ki % 4 == 0:
                            mk.wait(t, "T", sV,
                                    xh_toks[min(ki + 3, cfg.KC - 1)])
                        for m in range(nm):
                            mc = min(D, u.ncols - m * D)
                            last = t.matmul(
                                mk.ps[m][:, :],
                                ld.buf[:, ki * u.ncols + m * D:ki * u.ncols + m * D + mc],
                                sb["xhat"][:, ki * TILE:(ki + 1) * TILE],
                                start=(ki == 0), stop=(ki == cfg.KC - 1))
                    sT.inc(last)
                blk.tensor(f_tg_)
                g_tok2 = sT.n
                ws.mark_consumed(ld_g.slot, sT, g_tok2)
                ld_u = ws.load("e2", e, ug)

                def f_tu_(t, ld=ld_u, u=ug, nm=nm, pf=prev_u):
                    mk.wait(t, "T", sV, wbe8_tok)
                    mk.wait(t, "T", ld.sem, ld.tok)
                    if pf is not None:
                        mk.wait(t, "T", pf[0], pf[1])
                    last = None
                    for ki in range(cfg.KC):
                        for m in range(nm):
                            mc = min(D, u.ncols - m * D)
                            last = t.matmul(
                                mk.ps[4 + m][:, :],
                                ld.buf[:, ki * u.ncols + m * D:ki * u.ncols + m * D + mc],
                                sb["xhat"][:, ki * TILE:(ki + 1) * TILE],
                                start=(ki == 0), stop=(ki == cfg.KC - 1))
                    sT.inc(last)
                blk.tensor(f_tu_)
                u_tok = sT.n
                ws.mark_consumed(ld_u.slot, sT, u_tok)

                def f_si(s, g_tok2=g_tok2, nm=nm, pu=prev_u):
                    mk.wait(s, "S", sT, g_tok2)
                    if pu is not None:
                        mk.wait(s, "S", pu[0], pu[1])
                    for m in range(nm):
                        i = s.activation(out=ar(mk, 44 + m), in_=mk.ps[m][:, :],
                                         func=AF.Silu)
                    sS.inc(i)
                blk.scalar(f_si)
                si_tok = sS.n
                prev_g = (sS, si_tok)

                def f_h(v, si_tok=si_tok, u_tok=u_tok, nm=nm, mq=mq, ei=ei,
                        e=e):
                    mk.wait(v, "V", sS, si_tok)
                    mk.wait(v, "V", sT, u_tok)
                    mk.wait(v, "V", sV, wbe8_tok)
                    for m in range(nm):
                        i = v.tensor_tensor(ar(mk, 48 + m), ar(mk, 44 + m),
                                            mk.ps[4 + m][:, :], ALU.mult)
                    sV.inc(i)
                    mk.wait(v, "V", sV, sV.n)
                    for m in range(nm):
                        fc = mq * 4 + m
                        i = v.tensor_tensor(ar(mk, ei * cfg.KM + fc),
                                            ar(mk, 48 + m), ar(mk, 52 + e),
                                            ALU.mult)
                    sV.inc(i)
                blk.vector(f_h)
                prev_u = (sV, sV.n)
        he_tok = sV.n

        prev_free = [None, None]
        for mq in range(cfg.KC // 4):
            psbase = 4 * (mq % 2)
            pf = prev_free[mq % 2]
            for ei in range(4):
                e = half * 4 + ei
                u = units3[mq]
                ld = ws.load("e3", e, u)

                def f_t(t, ld=ld, u=u, ei=ei, psbase=psbase, pf=pf):
                    mk.wait(t, "T", sV, he_tok)
                    mk.wait(t, "T", ld.sem, ld.tok)
                    if ei == 0 and pf is not None:
                        mk.wait(t, "T", pf[0], pf[1])
                    last = None
                    for ki in range(cfg.KM):
                        for m in range(4):
                            last = t.matmul(
                                mk.ps[psbase + m][:, :],
                                ld.buf[:, ki * TILE + m * D:ki * TILE + (m + 1) * D],
                                ar(mk, ei * cfg.KM + ki),
                                start=(ei == 0 and ki == 0),
                                stop=(ei == 3 and ki == cfg.KM - 1))
                    sT.inc(last)
                blk.tensor(f_t)
                ws.mark_consumed(ld.slot, sT, sT.n)
                if half == 1 and mq == cfg.KC // 4 - 1 and ei == 0:
                    mk.p6_prev_user = [(sT, sT.n)]
            ttok_ = sT.n

            if half == 0:
                def f_v(v, mq=mq, psbase=psbase, ttok_=ttok_):
                    mk.wait(v, "V", sT, ttok_)
                    mk.wait(v, "V", mk.sems["sX3st"], x3st)
                    for m in range(4):
                        kc = mq * 4 + m
                        i = v.tensor_copy(
                            out=sb["Xres"][:, kc * TILE:(kc + 1) * TILE],
                            in_=mk.ps[psbase + m][:, :])
                    sV.inc(i)
                blk.vector(f_v)
            else:
                def f_v(v, mq=mq, psbase=psbase, ttok_=ttok_):
                    mk.wait(v, "V", sT, ttok_)
                    for m in range(4):
                        kc = mq * 4 + m
                        x = sb["Xres"][:, kc * TILE:(kc + 1) * TILE]
                        i = v.tensor_tensor(x, mk.ps[psbase + m][:, :], x,
                                            ALU.add)
                    sV.inc(i)
                blk.vector(f_v)
            prev_free[mq % 2] = (sV, sV.n)
    y_tok = sV.n

    for kc in range(cfg.KC):
        vcur = sV.n

        def f_ld(sync, kc=kc, vcur=vcur):
            mk.wait(sync, "SY", mk.sems["sX3st"], x3st)
            if kc >= 1:
                mk.wait(sync, "SY", sV, vcur)
            mk.sem("sX3L").inc(sync.dma_start(
                out=sb["fstage"][:],
                in_=mk.dram["X3"].ap()[b, tq, :, kc, :]), 16)
        blk.sync(f_ld)
        l_tok = mk.sems["sX3L"].n

        def f_add(v, kc=kc, l_tok=l_tok):
            mk.wait(v, "V", mk.sems["sX3L"], l_tok)
            mk.wait(v, "V", sV, y_tok)
            x = sb["Xres"][:, kc * TILE:(kc + 1) * TILE]
            sV.inc(v.tensor_tensor(x, sb["fstage"][:], x, ALU.add))
        blk.vector(f_add)
    out_tok = sV.n

    def f_st(sc):
        mk.wait(sc, "SC", sV, out_tok)
        mk.sem("sOutSt").inc(sc.dma_start(
            out=mk.dram["OUT"].ap()[b, tq],
            in_=sb["Xres"][:].rearrange("p (k t) -> p k t", k=cfg.KC)), 16)
    blk.scalar(f_st)



def build_program(cfg, phases):
    nc = bass.Bass("TRN2", target_bir_lowering=False)
    stack = contextlib.ExitStack()
    mk = MK(nc, cfg, stack)
    fused = len(phases) > 2
    with stack:
        wnames = sorted({w for p in phases for w in PHASE_W.get(p, [])})
        declare_weights(mk, wnames, need_g1=("p6" in phases),
                        need_bias=("p4" in phases or "p6" in phases))
        used = set()
        for p in phases:
            used.update(PHASE_IN.get(p, []))
            used.update(PHASE_OUT.get(p, []))
        if "p6" in phases:
            used.add("X3")
        kinds = {}
        for n in used:
            inn = any(n in PHASE_IN.get(p, []) for p in phases)
            outn = any(n in PHASE_OUT.get(p, []) for p in phases)
            produced_here = outn
            needed_later = inn and not produced_here
            if n == "OUT":
                kinds[n] = "ExternalOutput"
            elif n == "Xin":
                kinds[n] = "ExternalInput"
            elif n == "X3":
                kinds[n] = "Internal"
            elif fused:
                kinds[n] = "Internal"
            elif needed_later:
                kinds[n] = "ExternalInput"
            else:
                kinds[n] = "ExternalOutput"
        declare_scratch(mk, sorted(used), kinds)
        declare_consts(mk)
        declare_sbuf(mk)
        mk.sbuf("identb", [D, D], BF16)
        mk.psum_init()
        mk._rope_last_t = {}
        mk._rope_ps_free = {}
        with nc.Block() as blk:
            mk.stg = StageRing(mk, blk, 56, 4)
            emit_startup(mk, blk, phases)
            cs_names = {"sSY", "sG", "sV", "sT"}
            for eng, who in (("tensor", "T"), ("vector", "V"),
                             ("scalar", "S"), ("gpsimd", "P"), ("sync", "SY")):
                getattr(blk, eng)(lambda e, who=who: mk.wait_all(e, who, cs_names))
            mk.conv_marks = {}

            def conv_stage(names_):
                todo = [n for n in names_ if n in mk.wmeta]
                if todo:
                    emit_p0(mk, blk, todo)
            conv_stage(["q0", "k0", "v0"])
            tiles = [(b, tq) for b in range(cfg.B) for tq in range(cfg.TQ)]
            if "p1" in phases:
                for b, tq in tiles:
                    xht, xh = emit_ln(mk, blk, b, tq, "Xin", xbf_off=40, sq_off=0)
                    emit_qkv(mk, blk, 0, b, tq, xht, xh)
                    if tq == cfg.TQ - 1:
                        emit_bound(mk, blk, 0, b)
            if "p2" in phases:
                emit_fence(mk, blk)
            conv_stage(["o0", "gu0", "dn0"])
            if "p2" in phases:
                for b in range(cfg.B):
                    emit_negB_col(mk, blk, 0, b)
                    for h in range(0, cfg.H, 2):
                        emit_attn2(mk, blk, 0, b, h)
            if "p3" in phases:
                emit_fence(mk, blk)
                for b, tq in tiles:
                    emit_p3_tile(mk, blk, b, tq)
            if "p4" in phases:
                emit_fence(mk, blk)
            conv_stage(["q1", "k1", "v1"])
            if "p4" in phases:
                for b, tq in tiles:
                    xht, xh = emit_ln(mk, blk, b, tq, "X2", xbf_off=40, sq_off=0)
                    emit_qkv(mk, blk, 1, b, tq, xht, xh)
                    if tq == cfg.TQ - 1:
                        emit_bound(mk, blk, 1, b)
            if "p5" in phases:
                emit_fence(mk, blk)
            conv_stage(["o1", "e1", "e2", "e3"])
            if "p5" in phases:
                for b in range(cfg.B):
                    emit_negB_col(mk, blk, 1, b)
                    for h in range(0, cfg.NKV, 2):
                        emit_attn2(mk, blk, 1, b, h)
            if "p6" in phases:
                emit_fence(mk, blk)
                for b, tq in tiles:
                    emit_p6_tile(mk, blk, b, tq)

            def f_fin(sync):
                mk.wait_all(sync, "SYF")
            blk.sync(f_fin)
        mk.check_promises()
    return nc, mk



def pack_weights(cfg, inputs):
    perm = rope_perm(cfg.E)
    permk = rope_perm(cfg.CKV)
    out = {
        "wf_q0": host_block_weight(inputs["L0_attn_wq"]),
        "wf_k0": host_block_weight(inputs["L0_attn_wk"]),
        "wf_v0": host_block_weight(inputs["L0_attn_wv"]),
        "wf_o0": host_block_weight(inputs["L0_attn_wo"]),
        "wf_gu0": host_block_weight(np.asarray(inputs["L0_ffn_w_gate_up"]).T),
        "wf_dn0": host_block_weight(np.asarray(inputs["L0_ffn_w_down"]).T),
        "wf_q1": host_block_weight(np.asarray(inputs["L1_attn_Wq"])[:, perm]),
        "wf_k1": host_block_weight(np.asarray(inputs["L1_attn_Wk"])[:, permk]),
        "wf_v1": host_block_weight(inputs["L1_attn_Wv"]),
        "wf_o1": host_block_weight(inputs["L1_attn_Wo"]),
        "wf_g1": host_block_weight(np.asarray(inputs["L1_ffn_Wg"]).T,
                                   colq=cfg.NE),
        "bias_bq": np.asarray(inputs["L1_attn_bq"], dtype=np.float32)[perm][None, :],
        "bias_bk": np.asarray(inputs["L1_attn_bk"], dtype=np.float32)[permk][None, :],
        "bias_bv": np.asarray(inputs["L1_attn_bv"], dtype=np.float32)[None, :],
        "bias_bo": np.asarray(inputs["L1_attn_bo"], dtype=np.float32)[None, :],
    }
    for nm, key in (("e1", "L1_ffn_W1"), ("e2", "L1_ffn_W2"),
                    ("e3", "L1_ffn_W3")):
        w = np.asarray(inputs[key])
        out["wf_" + nm] = np.concatenate(
            [host_block_weight(w[e].T) for e in range(cfg.NE)], axis=1)
    return out


_BUILD_CACHE = {}


def kernel(**inputs):
    cfg = make_cfg()
    key = "fused"
    if key not in _BUILD_CACHE:
        _BUILD_CACHE[key] = build_program(cfg, PHASES_ALL)
    nc, mk = _BUILD_CACHE[key]
    wmap = pack_weights(cfg, inputs)
    x = np.asarray(inputs["L0_attn_x"], dtype=np.float32)
    in_maps = []
    for core in range(NCORES):
        m = dict(wmap)
        m["Xin"] = host_block_x(x[cfg.B * core:cfg.B * (core + 1)])
        in_maps.append(m)
    res = run_bass_kernel_spmd(nc, in_maps, core_ids=list(range(NCORES)))
    outs = [host_unblock_x(r["OUT"]) for r in res.results]
    return np.concatenate(outs, axis=0)


# version 13
# speedup vs baseline: 1.0568x; 1.0568x over previous
import contextlib
import math
from types import SimpleNamespace

import numpy as np

import concourse.bass as bass
import concourse.mybir as mybir
from concourse.bass_utils import run_bass_kernel_spmd

F32 = mybir.dt.float32
BF16 = mybir.dt.bfloat16
AF = mybir.ActivationFunctionType
ALU = mybir.AluOpType
AXX = mybir.AxisListType.X

D = 128
TILE = 512
NCORES = 8


def make_cfg(E=2048, T=2048, B=2, DFF=8192, NKV=4, NE=8, DM=1408):
    c = SimpleNamespace()
    c.E, c.T, c.B, c.DFF, c.NKV, c.NE, c.DM = E, T, B, DFF, NKV, NE, DM
    c.KC = E // D
    c.H = E // D
    c.G = c.H // NKV
    c.CKV = NKV * D
    c.KCV = c.CKV // D
    c.TQ = T // TILE
    c.NT = c.B * c.TQ
    c.KFF = DFF // D
    c.KM = DM // D
    c.scale = 1.0 / math.sqrt(D)
    c.eps = 1e-5
    return c


PHASES_ALL = ["p0", "p1", "p2", "p3", "p4", "p5", "p6"]

PHASE_W = {
    "p1": ["q0", "k0", "v0"],
    "p3": ["o0", "gu0", "dn0"],
    "p4": ["q1", "k1", "v1"],
    "p6": ["o1", "e1", "e2", "e3"],
}
PHASE_IN = {
    "p1": ["Xin"], "p2": ["Q0", "K0", "V0", "NB"], "p3": ["Xin", "O0"],
    "p4": ["X2"], "p5": ["QE1", "K1", "V1", "NB"], "p6": ["X2", "O1"],
}
PHASE_OUT = {
    "p1": ["Q0", "K0", "V0", "NB"], "p2": ["O0"], "p3": ["X2"],
    "p4": ["QE1", "K1", "V1", "NB"], "p5": ["O1"], "p6": ["OUT"],
}



def weight_plan(K, M, colq=TILE, kcmax=16):
    KCn = K // D
    units = []
    off = 0
    for col0 in range(0, M, colq):
        ncols = min(colq, M - col0)
        for kc0 in range(0, KCn, kcmax):
            nkc = min(kcmax, KCn - kc0)
            units.append(SimpleNamespace(off=off, kc0=kc0, nkc=nkc,
                                         col0=col0, ncols=ncols))
            off += nkc * ncols
    return units, off


def host_block_weight(w, colq=TILE):
    w = np.asarray(w, dtype=np.float32)
    K, M = w.shape
    units, tot = weight_plan(K, M, colq=colq)
    out = np.empty((D, tot), dtype=np.float32)
    for u in units:
        sub = w[u.kc0 * D:(u.kc0 + u.nkc) * D, u.col0:u.col0 + u.ncols]
        blk = sub.reshape(u.nkc, D, u.ncols).transpose(1, 0, 2).reshape(D, -1)
        out[:, u.off:u.off + u.nkc * u.ncols] = blk
    return out


def host_block_x(xc):
    B, T, E = xc.shape
    y = np.asarray(xc, dtype=np.float32).reshape(B, T // TILE, TILE, E // D, D)
    return np.ascontiguousarray(y.transpose(0, 1, 4, 3, 2))


def host_unblock_x(y):
    B, TQ, P, KC, C = y.shape
    return np.ascontiguousarray(
        y.transpose(0, 1, 4, 3, 2).reshape(B, TQ * C, KC * P))


def rope_perm(E):
    perm = np.arange(E).reshape(-1, D)
    half = np.concatenate([np.arange(0, D, 2), np.arange(1, D, 2)])
    return (perm[:, half]).reshape(-1)


def host_consts(cfg):
    T = cfg.T
    pos = np.arange(T, dtype=np.float64)
    inv = 10000.0 ** (-np.arange(64, dtype=np.float64) / 64.0)
    ang = inv[:, None] * pos[None, :]
    cos = np.cos(ang).astype(np.float32)
    sin = np.sin(ang).astype(np.float32)
    cos2 = np.concatenate([cos, cos], axis=0)
    sin2 = np.concatenate([sin, sin], axis=0)
    R = np.zeros((D, D), dtype=np.float32)
    R[0:64, 64:128] = -np.eye(64)
    R[64:128, 0:64] = np.eye(64)
    RT = np.ascontiguousarray(R.T)
    masks = np.zeros((4, D, TILE), dtype=np.float32)
    for j in range(4):
        kk = np.arange(D)[:, None] + D * j
        qq = np.arange(TILE)[None, :]
        masks[j] = (kk <= qq).astype(np.float32)
    ident = np.eye(D, dtype=np.float32)
    E16 = np.zeros((D, cfg.KC * cfg.KC), dtype=np.float32)
    for kc in range(cfg.KC):
        E16[:, kc * cfg.KC + kc] = 1.0
    E4 = np.zeros((D, cfg.KCV * cfg.KCV), dtype=np.float32)
    for kc in range(cfg.KCV):
        E4[:, kc * cfg.KCV + kc] = 1.0
    return dict(cos=cos2, sin=sin2, ropeRT=RT,
                masks=masks.reshape(4 * D, TILE), ident=ident, E16=E16, E4=E4)



class Sem:
    def __init__(self, stack, nc, name):
        self.h = stack.enter_context(nc.semaphore(name))
        self.n = 0
        self.name = name

    def inc(self, instr, k=1):
        instr.then_inc(self.h, k)
        self.n += k
        return self.n


class MK:
    def __init__(self, nc, cfg, stack):
        self.nc = nc
        self.cfg = cfg
        self.stack = stack
        self.dram = {}
        self.sb = {}
        self.ps = []
        self.sems = {}
        self._waited = {}
        self._last_emitter = {}
        self._promises = []
        self.wmeta = {}
        self.attn_state = {}

    def sem(self, name):
        if name not in self.sems:
            self.sems[name] = Sem(self.stack, self.nc, name)
        return self.sems[name]

    SELF = {"T": "sT", "V": "sV", "S": "sS", "P": "sP"}

    def wait(self, proxy, who, sem, val, future=False, force=False):
        if val <= 0:
            return
        key = (who, sem.name)
        force = force or (self.SELF.get(who) == sem.name)
        if not force and self._waited.get(key, -1) >= val:
            return
        if force and self._waited.get(key, -1) >= val and \
                self._last_emitter.get(who) == key:
            return
        if not future:
            assert val <= sem.n, \
                f"wait {who} {sem.name}>={val} but only {sem.n} produced"
        else:
            self._promises.append((who, sem.name, val))
        proxy.wait_ge(sem.h, val)
        self._waited[key] = val
        self._last_emitter[who] = key

    def wait_all(self, proxy, who, names=None):
        for nm, s in list(self.sems.items()):
            if names is not None and nm not in names:
                continue
            self.wait(proxy, who, s, s.n, force=True)

    def check_promises(self):
        for who, semname, val in self._promises:
            s = self.sems[semname]
            assert val <= s.n, f"promised {who} {semname}>={val}, produced {s.n}"
        self._promises = []

    def dram_t(self, name, shape, dtype, kind):
        h = self.nc.dram_tensor(name, list(shape), dtype, kind=kind)
        self.dram[name] = h
        return h

    def sbuf(self, name, shape, dtype):
        h = self.stack.enter_context(self.nc.sbuf_tensor(name, list(shape), dtype))
        self.sb[name] = h
        return h

    def psum_init(self):
        self.psp = []
        self.psp.append(self.stack.enter_context(
            self.nc.psum_tensor("pspA", [D, 2 * TILE], F32)))
        for i in range(2, 6):
            self.ps.append(self.stack.enter_context(
                self.nc.psum_tensor(f"psb{i}", [D, TILE], F32)))
        self.psp.append(self.stack.enter_context(
            self.nc.psum_tensor("pspB", [D, 2 * TILE], F32)))
        pa, pb = self.psp
        self.ps = [pa[:, 0:TILE], pa[:, TILE:2 * TILE]] + \
            [t[:, :] for t in self.ps] + \
            [pb[:, 0:TILE], pb[:, TILE:2 * TILE]]


def wspecs(cfg):
    E = cfg.E
    return {
        "q0": (E, E, 1), "k0": (E, E, 1), "v0": (E, E, 1), "o0": (E, E, 1),
        "gu0": (E, 2 * cfg.DFF, 1), "dn0": (cfg.DFF, E, 1),
        "q1": (E, E, 1), "k1": (E, cfg.CKV, 1), "v1": (E, cfg.CKV, 1),
        "o1": (cfg.CKV, E, 1),
        "e1": (E, cfg.DM, cfg.NE), "e2": (E, cfg.DM, cfg.NE),
        "e3": (cfg.DM, E, cfg.NE),
    }


def declare_weights(mk, names, need_g1, need_bias):
    cfg = mk.cfg
    allspecs = wspecs(cfg)
    for name in names:
        K, M, nrep = allspecs[name]
        units, tot = weight_plan(K, M)
        mk.wmeta[name] = (units, tot, nrep)
        mk.dram_t("wf_" + name, (D, nrep * tot), F32, "ExternalInput")
        mk.dram_t("wb_" + name, (D, nrep * tot), BF16, "Internal")
    if "q1" in names:
        units, tot = weight_plan(cfg.E, cfg.CKV)
        mk.wmeta["qe"] = (units, tot, 1)
        mk.dram_t("wb_qe", (D, tot), BF16, "Internal")
    if need_g1:
        units, tot = weight_plan(cfg.E, cfg.NE, colq=cfg.NE)
        mk.wmeta["g1"] = (units, tot, 1)
        mk.dram_t("wf_g1", (D, tot), F32, "ExternalInput")
    if need_bias:
        for name, n in (("bq", cfg.E), ("bk", cfg.CKV), ("bv", cfg.CKV),
                        ("bo", cfg.E)):
            mk.dram_t("bias_" + name, (1, n), F32, "ExternalInput")


def scratch_shapes(cfg):
    return {
        "Xin": ((cfg.B, cfg.TQ, D, cfg.KC, TILE), F32),
        "X2": ((cfg.B, cfg.TQ, D, cfg.KC, TILE), F32),
        "X3": ((cfg.B, cfg.TQ, D, cfg.KC, TILE), F32),
        "OUT": ((cfg.B, cfg.TQ, D, cfg.KC, TILE), F32),
        "Q0": ((cfg.B, cfg.H, D, cfg.T), BF16),
        "K0": ((cfg.B, cfg.H, D, cfg.T), BF16),
        "V0": ((cfg.B, cfg.H, cfg.T, D), BF16),
        "O0": ((cfg.B, cfg.H, D, cfg.T), BF16),
        "QE1": ((cfg.B, cfg.NKV, D, cfg.T), BF16),
        "K1": ((cfg.B, cfg.NKV, D, cfg.T), BF16),
        "V1": ((cfg.B, cfg.NKV, cfg.T, D), BF16),
        "O1": ((cfg.B, cfg.NKV, D, cfg.T), BF16),
        "NB": ((2, cfg.B), F32),
    }


def declare_scratch(mk, names, kinds):
    shapes = scratch_shapes(mk.cfg)
    for n in names:
        sh, dt = shapes[n]
        mk.dram_t(n, sh, dt, kinds.get(n, "Internal"))


def declare_consts(mk):
    for name, arr in host_consts(mk.cfg).items():
        mk.dram["c_" + name] = mk.nc.inline_tensor(
            np.ascontiguousarray(arr), "c_" + name)


AR_N = 68


def ar(mk, t0, nt=1):
    return mk.sb["arena"][:, t0 * TILE:(t0 + nt) * TILE]


def declare_sbuf(mk):
    cfg = mk.cfg
    KCT = cfg.KC * TILE
    mk.sbuf("xhat", [D, KCT], BF16)
    mk.sbuf("Xres", [D, KCT], F32)
    mk.sbuf("arena", [D, AR_N * TILE], BF16)
    mk.sbuf("wsA", [D, 8192], BF16)
    mk.sbuf("wsB", [D, 8192], BF16)
    mk.sbuf("ones_row", [1, TILE], BF16)
    mk.sbuf("ones_col", [D, 1], BF16)
    mk.sbuf("onesf", [D, 1], F32)
    mk.sbuf("ident_f", [D, D], F32)
    mk.sbuf("ropeRT", [D, D], BF16)
    mk.sbuf("masks", [D, 4 * TILE], BF16)
    mk.sbuf("cosb", [D, cfg.T], BF16)
    mk.sbuf("sinb", [D, cfg.T], BF16)
    mk.sbuf("E16", [D, cfg.KC * cfg.KC], BF16)
    mk.sbuf("E4", [D, cfg.KCV * cfg.KCV], BF16)
    mk.sbuf("brows", [1, 2 * cfg.E + 2 * cfg.CKV], BF16)
    mk.sbuf("wg", [D, cfg.KC * cfg.NE], F32)
    mk.sbuf("csum", [1, cfg.NE], F32)
    mk.sbuf("stats", [1, 5 * TILE], F32)
    mk.sbuf("rowsb", [1, 2 * TILE], BF16)
    mk.sbuf("nrm", [cfg.KC, 4], F32)
    mk.sbuf("negB", [D, 2], F32)
    mk.sbuf("smal", [D, 128], F32)
    mk.sbuf("g8", [cfg.NE, TILE], F32)
    mk.sbuf("g8s", [cfg.NE, TILE], F32)
    mk.sbuf("wBe", [D, TILE], BF16)
    mk.sbuf("fstage", [D, TILE], F32)
    mk.sbuf("fstage2", [D, TILE], F32)



def emit_fence(mk, blk):
    sems = list(mk.sems.values())
    fence = [(s, s.n) for s in sems]

    def mkf(who):
        def f(e):
            for s, v in fence:
                mk.wait(e, who, s, v)
        return f
    blk.tensor(mkf("T"))
    blk.vector(mkf("V"))
    blk.scalar(mkf("S"))
    blk.gpsimd(mkf("P"))
    blk.sync(mkf("SY"))


def emit_startup(mk, blk, phases):
    cfg = mk.cfg
    sG = mk.sem("sG")
    sV = mk.sem("sV")
    sT = mk.sem("sT")
    c, sb = mk.dram, mk.sb

    def f_sync(sync):
        mk.sem("sSY").inc(
            sync.dma_start(out=sb["ident_f"][:], in_=c["c_ident"].ap()), 16)
    blk.sync(f_sync)

    def f_g(g):
        casts = [
            (sb["identb"][:], c["c_ident"].ap()),
            (sb["ropeRT"][:], c["c_ropeRT"].ap()),
            (sb["masks"][:].rearrange("p (j t) -> p j t", j=4),
             c["c_masks"].ap().rearrange("(j p) t -> p j t", p=D)),
            (sb["cosb"][:], c["c_cos"].ap()),
            (sb["sinb"][:], c["c_sin"].ap()),
            (sb["E16"][:], c["c_E16"].ap()),
            (sb["E4"][:], c["c_E4"].ap()),
        ]
        if "bias_bq" in mk.dram:
            casts += [
                (sb["brows"][0:1, 0:cfg.E], c["bias_bq"].ap()),
                (sb["brows"][0:1, cfg.E:2 * cfg.E], c["bias_bo"].ap()),
                (sb["brows"][0:1, 2 * cfg.E:2 * cfg.E + cfg.CKV], c["bias_bk"].ap()),
                (sb["brows"][0:1, 2 * cfg.E + cfg.CKV:], c["bias_bv"].ap()),
            ]
        for o, i in casts:
            sG.inc(g.dma_start(out=o, in_=i), 16)
        if "wf_g1" in mk.dram:
            sG.inc(g.dma_start(out=sb["wg"][:], in_=c["wf_g1"].ap()), 16)
    blk.gpsimd(f_g)

    def f_v(v):
        sV.inc(v.memset(sb["ones_row"][:], 1.0))
        sV.inc(v.memset(sb["ones_col"][:], 1.0))
        sV.inc(v.memset(sb["onesf"][:], 1.0))
        sV.inc(v.memset(sb["nrm"][:], 0.0))
    blk.vector(f_v)

    if "wf_g1" in mk.dram:
        gtok, vtok = sG.n, sV.n

        def f_t(t):
            mk.wait(t, "T", sG, gtok)
            mk.wait(t, "T", sV, vtok)
            for kc in range(cfg.KC):
                i = t.matmul(mk.ps[6][0:1, 0:cfg.NE],
                             sb["onesf"][:],
                             sb["wg"][:, kc * cfg.NE:(kc + 1) * cfg.NE],
                             start=(kc == 0), stop=(kc == cfg.KC - 1))
            sT.inc(i)
        blk.tensor(f_t)
        ttok = sT.n

        def f_v2(v):
            mk.wait(v, "V", sT, ttok)
            sV.inc(v.tensor_copy(out=sb["csum"][:],
                                 in_=mk.ps[6][0:1, 0:cfg.NE]))
        blk.vector(f_v2)


def emit_p0(mk, blk, names):
    CH = 1 << 15

    def f(g):
        for name in names:
            if name == "g1" or name not in mk.wmeta:
                continue
            sW = mk.sem("sWc_" + name)
            units, tot, nrep = mk.wmeta[name]
            total = tot * nrep
            srcap = mk.dram["wf_" + name].ap()
            dstap = mk.dram["wb_" + name].ap()
            for o0 in range(0, total, CH):
                o1 = min(total, o0 + CH)
                sW.inc(g.dma_start(out=dstap[:, o0:o1], in_=srcap[:, o0:o1]), 16)
            mk.conv_marks[name] = (sW, sW.n)
    blk.gpsimd(f)



class WStream:
    def __init__(self, mk, blk):
        self.mk = mk
        self.blk = blk
        self.slots = [mk.sb["wsA"], mk.sb["wsB"]]
        self.ssem = [mk.sem("sWs0"), mk.sem("sWs1")]
        st = getattr(mk, "ws_state", None)
        if st is None:
            st = mk.ws_state = {"i": 0, "count": [0, 0], "consumed": [None, None]}
        self.i = st["i"]
        self.count = st["count"]
        self.consumed = st["consumed"]

    def _save(self):
        self.mk.ws_state["i"] = self.i

    def load(self, name, rep, u):
        mk = self.mk
        s = self.i % 2
        self.i += 1
        cons = self.consumed[s]
        _, tot, _ = mk.wmeta[name]
        base = rep * tot + u.off
        width = u.nkc * u.ncols
        apw = mk.dram["wb_" + name].ap()[:, base:base + width]
        conv = getattr(mk, "conv_marks", {}).get(name)

        def f(sync):
            if conv:
                mk.wait(sync, "SY", conv[0], conv[1])
            if cons is not None:
                mk.wait(sync, "SY", cons[0], cons[1])
            self.ssem[s].inc(
                sync.dma_start(out=self.slots[s][:, 0:width], in_=apw), 16)
        self.blk.sync(f)
        self.count[s] += 1
        self._save()
        return SimpleNamespace(buf=self.slots[s], slot=s, sem=self.ssem[s],
                               tok=16 * self.count[s], u=u)

    def mark_consumed(self, s, sem, val):
        self.consumed[s] = (sem, val)
        self._save()


class StageRing:
    def __init__(self, mk, blk, base, n):
        self.mk = mk
        self.blk = blk
        self.base = base
        self.n = n
        self.sems = [mk.sem(f"sStg{i}") for i in range(n)]
        self.uses = [0] * n
        self.i = 0

    def next(self):
        s = self.i % self.n
        self.i += 1
        return s

    def slot_ap(self, s):
        return ar(self.mk, self.base + s)

    def wait_free(self, proxy, who, s):
        if self.uses[s] > 0:
            self.mk.wait(proxy, who, self.sems[s], 16 * self.uses[s])

    def store(self, s, dst_ap, prod_sem, prod_tok):
        mk = self.mk

        def f(sc):
            mk.wait(sc, "SC", prod_sem, prod_tok)
            self.sems[s].inc(sc.dma_start(out=dst_ap, in_=self.slot_ap(s)), 16)
        self.blk.scalar(f)
        self.uses[s] += 1



def emit_ln(mk, blk, b, tq, src_dram, xbf_off, sq_off, xbf_ready=None):
    cfg = mk.cfg
    sG, sP = mk.sem("sG"), mk.sem("sP")
    sS, sV, sT = mk.sem("sS"), mk.sem("sV"), mk.sem("sT")
    sb = mk.sb
    xbf = ar(mk, xbf_off, cfg.KC)

    if src_dram is not None:
        prevu = getattr(mk, "xbf_prev_user", None)
        if prevu is None:
            prevu = [(sV, sV.n), (sT, sT.n), (sS, sS.n), (sP, sP.n)]
        sXF = [mk.sem("sXF0"), mk.sem("sXF1")]
        stage = [sb["fstage"], sb["fstage2"]]
        fs_user = getattr(mk, "fstage_user", None)
        ld_toks = {}

        vbase = sV.n

        def f_g(sync):
            for sm_, tk_ in prevu:
                mk.wait(sync, "SY", sm_, tk_)
            if fs_user is not None:
                mk.wait(sync, "SY", fs_user[0], fs_user[1])
            for kc in range(cfg.KC):
                if kc >= 2:
                    mk.wait(sync, "SY", sV, vbase + (kc - 2) + 1, future=True)
                sXF[kc % 2].inc(
                    sync.dma_start(out=stage[kc % 2][:],
                                   in_=mk.dram[src_dram].ap()[b, tq, :, kc, :]), 16)
                ld_toks[kc] = sXF[kc % 2].n
        cast_toks = {}

        def f_cst(v):
            mk.wait(v, "V", sV, sV.n)
            for kc in range(cfg.KC):
                mk.wait(v, "V", sXF[kc % 2], ld_toks[kc])
                sV.inc(v.tensor_copy(out=xbf[:, kc * TILE:(kc + 1) * TILE],
                                     in_=stage[kc % 2][:]))
                cast_toks[kc] = sV.n
        blk.sync(f_g)
        blk.vector(f_cst)
        mk.fstage_user = (sV, sV.n)
        xbf_ready = (sV, sV.n)

    vtok0, ttok0 = sV.n, sT.n
    sqV, sqP = {}, {}

    def f_sq_v(v):
        mk.wait(v, "V", xbf_ready[0], xbf_ready[1])
        mk.wait(v, "V", sT, ttok0)
        mk.wait(v, "V", sV, sV.n)
        for kc in range(0, cfg.KC, 2):
            sV.inc(v.tensor_tensor(ar(mk, sq_off + kc),
                                   xbf[:, kc * TILE:(kc + 1) * TILE],
                                   xbf[:, kc * TILE:(kc + 1) * TILE], ALU.mult))
            sqV[kc] = sV.n
    blk.vector(f_sq_v)

    def f_sq_p(p):
        mk.wait(p, "P", xbf_ready[0], xbf_ready[1])
        mk.wait(p, "P", sT, ttok0)
        mk.wait(p, "P", sV, vtok0)
        mk.wait(p, "P", sP, sP.n)
        for kc in range(1, cfg.KC, 2):
            sP.inc(p.tensor_tensor(ar(mk, sq_off + kc),
                                   xbf[:, kc * TILE:(kc + 1) * TILE],
                                   xbf[:, kc * TILE:(kc + 1) * TILE], ALU.mult))
            sqP[kc] = sP.n
    blk.gpsimd(f_sq_p)

    stok0 = sS.n

    def f_t(t):
        mk.wait(t, "T", xbf_ready[0], xbf_ready[1])
        mk.wait(t, "T", sS, stok0)
        for kc in range(cfg.KC):
            if kc % 2 == 0:
                mk.wait(t, "T", sV, sqV[kc])
            else:
                mk.wait(t, "T", sP, sqP[kc])
            t.matmul(mk.ps[6][0:1, :], mk.sb["ones_col"][:],
                     xbf[:, kc * TILE:(kc + 1) * TILE],
                     start=(kc == 0), stop=(kc == cfg.KC - 1))
            i = t.matmul(mk.ps[7][0:1, :], mk.sb["ones_col"][:],
                         ar(mk, sq_off + kc),
                         start=(kc == 0), stop=(kc == cfg.KC - 1))
        sT.inc(i)
    blk.tensor(f_t)
    ttok1 = sT.n

    stats = sb["stats"]
    mean = stats[0:1, 0:TILE]
    ex2 = stats[0:1, TILE:2 * TILE]
    var = stats[0:1, 2 * TILE:3 * TILE]
    rstd = stats[0:1, 3 * TILE:4 * TILE]
    brow = stats[0:1, 4 * TILE:5 * TILE]
    abf = sb["rowsb"][0:1, 0:TILE]
    bbf = sb["rowsb"][0:1, TILE:2 * TILE]
    rinv = 1.0 / cfg.E

    def f_v(v):
        mk.wait(v, "V", sT, ttok1)
        sV.inc(v.tensor_scalar_mul(mean, mk.ps[6][0:1, :], rinv))
        sV.inc(v.tensor_scalar_mul(ex2, mk.ps[7][0:1, :], rinv))
        mk.wait(v, "V", sV, sV.n)
        sV.inc(v.scalar_tensor_tensor(var, mean, -1.0, mean, ALU.mult, ALU.mult))
        mk.wait(v, "V", sV, sV.n)
        sV.inc(v.tensor_add(var, var, ex2))
        mk.wait(v, "V", sV, sV.n)
        sV.inc(v.tensor_scalar_add(var, var, cfg.eps))
    blk.vector(f_v)
    vtok1 = sV.n

    def f_s(s):
        mk.wait(s, "S", sV, vtok1)
        sS.inc(s.activation(out=var, in_=var, func=AF.Ln))
        mk.wait(s, "S", sS, sS.n)
        sS.inc(s.activation(out=rstd, in_=var, func=AF.Exp, scale=-0.5))
    blk.scalar(f_s)
    stok1 = sS.n

    def f_v2(v):
        mk.wait(v, "V", sS, stok1)
        sV.inc(v.scalar_tensor_tensor(brow, mean, -1.0, rstd, ALU.mult, ALU.mult))
        mk.wait(v, "V", sV, sV.n)
        sV.inc(v.tensor_copy(out=abf, in_=rstd))
        sV.inc(v.tensor_copy(out=bbf, in_=brow))
    blk.vector(f_v2)
    vtok2 = sV.n

    def f_t2(t):
        mk.wait(t, "T", sV, vtok2)
        t.matmul(mk.ps[6][:, :], mk.sb["ones_row"][0:1, 0:D], abf,
                 start=True, stop=True)
        sT.inc(t.matmul(mk.ps[7][:, :], mk.sb["ones_row"][0:1, 0:D], bbf,
                        start=True, stop=True))
    blk.tensor(f_t2)
    ttok2 = sT.n

    aB = ar(mk, 60)
    bB = ar(mk, 61)

    def f_s2(v):
        mk.wait(v, "V", sT, ttok2)
        mk.wait(v, "V", sP, sP.n)
        mk.wait(v, "V", sV, sV.n)
        sV.inc(v.tensor_copy(out=aB, in_=mk.ps[6][:, :]))
        sV.inc(v.tensor_copy(out=bB, in_=mk.ps[7][:, :]))
    blk.vector(f_s2)
    stok2 = sV.n

    chunk_toks = {}

    def f_v3(v):
        mk.wait(v, "V", sV, stok2)
        tmp = ar(mk, 62)
        for kc in range(cfg.KC):
            mk.wait(v, "V", sV, sV.n)
            i1 = v.tensor_tensor(tmp, xbf[:, kc * TILE:(kc + 1) * TILE],
                                 aB, ALU.mult)
            sV.inc(i1)
            mk.wait(v, "V", sV, sV.n)
            i2 = v.tensor_tensor(mk.sb["xhat"][:, kc * TILE:(kc + 1) * TILE],
                                 tmp, bB, ALU.add)
            sV.inc(i2)
            chunk_toks[kc] = sV.n
    blk.vector(f_v3)
    if src_dram is not None:
        mk.xbf_prev_user = [(sV, sV.n)]
    return chunk_toks, sV.n



def fm_proj(mk, blk, ws, wname, rep, rhs, rhs_sem, rhs_tok, kclist, mqs,
            evac, bias_row_off=None, psmode="ping", rhs_toks=None):
    cfg = mk.cfg
    sT = mk.sem("sT")
    units, tot, _ = mk.wmeta[wname]
    prev_free = [None, None]

    for gi, mq in enumerate(mqs):
        u = units[mq]
        assert u.nkc == len(kclist), (wname, mq, u.nkc, len(kclist))
        ld = ws.load(wname, rep, u)
        nm = (u.ncols + D - 1) // D
        gslot = (gi % 2) if psmode == "ping" else 0
        psbase = 4 * gslot
        pf = prev_free[gslot]

        def f_t(t, ld=ld, nm=nm, psbase=psbase, pf=pf, u=u, gi=gi):
            first = gi == 0 and rhs_toks is not None
            if not first:
                mk.wait(t, "T", rhs_sem, rhs_tok)
            mk.wait(t, "T", ld.sem, ld.tok)
            if pf is not None:
                mk.wait(t, "T", pf[0], pf[1])
            last = None
            nk = len(kclist)
            bias = bias_row_off is not None
            for ki, kc in enumerate(kclist):
                if first and ki % 4 == 0:
                    mk.wait(t, "T", rhs_sem,
                            rhs_toks[min(kc + 3, kclist[-1])])
                r = rhs(kc)
                for m in range(nm):
                    mc = min(D, u.ncols - m * D)
                    last = t.matmul(
                        mk.ps[psbase + m][:, 0:r.shape[-1]],
                        ld.buf[:, ki * u.ncols + m * D:ki * u.ncols + m * D + mc],
                        r, start=(ki == 0),
                        stop=(ki == nk - 1 and not bias))
            if bias:
                for m in range(nm):
                    mc = min(D, u.ncols - m * D)
                    co = bias_row_off + u.col0 + m * D
                    last = t.matmul(
                        mk.ps[psbase + m][:, 0:TILE],
                        mk.sb["brows"][0:1, co:co + mc],
                        mk.sb["ones_row"][0:1, :],
                        start=False, stop=True)
            sT.inc(last)
        blk.tensor(f_t)
        ttok = sT.n
        ws.mark_consumed(ld.slot, sT, ttok)
        prev_free[gslot] = evac(blk, mq, psbase, nm, ttok)



def emit_rope(mk, blk, raw_tok_sem, raw_tok, raw, tq, psidx, slot, eng="V"):
    cfg = mk.cfg
    sE = mk.sem("s" + eng)
    sT = mk.sem("sT")
    xc = ar(mk, 32 + 2 * slot)
    xs = ar(mk, 33 + 2 * slot)
    cs = mk.sb["cosb"][:, tq * TILE:(tq + 1) * TILE]
    sn = mk.sb["sinb"][:, tq * TILE:(tq + 1) * TILE]

    def f_e(e):
        mk.wait(e, eng, raw_tok_sem, raw_tok)
        mk.wait(e, eng, sT, mk._rope_last_t.get(slot, 0))
        sE.inc(e.tensor_tensor(xc, raw, cs, ALU.mult))
        sE.inc(e.tensor_tensor(xs, raw, sn, ALU.mult))
    (blk.vector if eng == "V" else blk.gpsimd)(f_e)
    etok = sE.n

    ps_free = mk._rope_ps_free.get(psidx)

    def f_t(t):
        mk.wait(t, "T", sE, etok)
        if ps_free is not None:
            mk.wait(t, "T", ps_free[0], ps_free[1])
        t.matmul(mk.ps[psidx][:, :], mk.sb["ropeRT"][:, :], xs,
                 start=True, stop=False)
        sT.inc(t.matmul(mk.ps[psidx][:, :], mk.sb["identb"][:, :], xc,
                        start=False, stop=True))
    blk.tensor(f_t)
    mk._rope_last_t[slot] = sT.n
    return sT.n



def emit_norms(mk, blk, col, src_base, nch):
    cfg = mk.cfg
    sT, sV, sP = mk.sem("sT"), mk.sem("sV"), mk.sem("sP")
    nE = cfg.KC if nch == cfg.KC else cfg.KCV
    Emat = mk.sb["E16"] if nch == cfg.KC else mk.sb["E4"]
    ptok0 = sP.n
    tbase = sT.n
    vtok0 = sV.n

    def f_p(p):
        mk.wait(p, "P", sV, vtok0)
        for i in range(nch):
            if i >= 2:
                mk.wait(p, "P", sT, tbase + (i - 1), future=True)
            src = ar(mk, src_base + i)
            sP.inc(p.tensor_tensor(ar(mk, 62 + (i % 2)), src, src, ALU.mult))
    blk.gpsimd(f_p)

    vcur0 = sV.n

    def f_t(t):
        mk.wait(t, "T", sV, vcur0)
        for i in range(nch):
            mk.wait(t, "T", sP, ptok0 + i + 1)
            sT.inc(t.matmul(mk.ps[2][0:nE, :],
                            Emat[:, i * nE:(i + 1) * nE],
                            ar(mk, 62 + (i % 2)),
                            start=(i == 0), stop=(i == nch - 1)))
    blk.tensor(f_t)
    ttok = sT.n

    def f_v(v):
        mk.wait(v, "V", sT, ttok)
        sV.inc(v.reduce_max(mk.sb["smal"][0:nE, 32:33], mk.ps[2][0:nE, :],
                            axis=AXX))
        mk.wait(v, "V", sV, sV.n)
        sV.inc(v.tensor_tensor(mk.sb["nrm"][0:nE, col:col + 1],
                               mk.sb["nrm"][0:nE, col:col + 1],
                               mk.sb["smal"][0:nE, 32:33], ALU.max))
    blk.vector(f_v)


def emit_bound(mk, blk, layer, b):
    cfg = mk.cfg
    sT, sV, sS = mk.sem("sT"), mk.sem("sV"), mk.sem("sS")
    nq = cfg.KC if layer == 0 else cfg.KCV
    c0, c1 = (0, 1) if layer == 0 else (2, 3)
    vtok = sV.n
    sm = mk.sb["smal"]

    def f_t(t):
        mk.wait(t, "T", sV, vtok)
        t.transpose(mk.ps[5][0:1, 0:nq], mk.sb["nrm"][0:nq, c0:c0 + 1],
                    mk.sb["ident_f"][0:nq, 0:nq])
        mk.sems["sT"].inc(
            t.transpose(mk.ps[4][0:1, 0:nq], mk.sb["nrm"][0:nq, c1:c1 + 1],
                        mk.sb["ident_f"][0:nq, 0:nq]))
    blk.tensor(f_t)
    ttok = sT.n

    def f_v(v):
        mk.wait(v, "V", sT, ttok)
        sV.inc(v.reduce_max(sm[0:1, 34:35], mk.ps[5][0:1, 0:nq], axis=AXX))
        sV.inc(v.reduce_max(sm[0:1, 35:36], mk.ps[4][0:1, 0:nq], axis=AXX))
        mk.wait(v, "V", sV, sV.n)
        sV.inc(v.tensor_tensor(sm[0:1, 36:37], sm[0:1, 34:35],
                               sm[0:1, 35:36], ALU.mult))
        sV.inc(v.memset(mk.sb["nrm"][0:nq, c0:c0 + 1], 0.0))
        sV.inc(v.memset(mk.sb["nrm"][0:nq, c1:c1 + 1], 0.0))
    blk.vector(f_v)
    vtok2 = sV.n

    def f_s(s):
        mk.wait(s, "S", sV, vtok2)
        sS.inc(s.activation(out=sm[0:1, 37:38], in_=sm[0:1, 36:37], func=AF.Ln))
        mk.wait(s, "S", sS, sS.n)
        sS.inc(s.activation(out=sm[0:1, 38:39], in_=sm[0:1, 37:38],
                            func=AF.Exp, scale=0.5))
    blk.scalar(f_s)
    stok = sS.n

    def f_v2(v):
        mk.wait(v, "V", sS, stok)
        sNBp = mk.sem("sNB")
        mk.wait(v, "V", sNBp, sNBp.n)
        sV.inc(v.tensor_scalar_mul(sm[0:1, 39:40], sm[0:1, 38:39], -cfg.scale))
    blk.vector(f_v2)
    vtok3 = sV.n

    def f_st(sc):
        mk.wait(sc, "SC", sV, vtok3)
        mk.sem("sNB").inc(
            sc.dma_start(out=mk.dram["NB"].ap()[layer:layer + 1, b:b + 1],
                         in_=sm[0:1, 39:40]), 16)
    blk.scalar(f_st)


def emit_negB_col(mk, blk, layer, b):
    sV, sT, sS = mk.sem("sV"), mk.sem("sT"), mk.sem("sS")
    sL = mk.sem("sNBL")
    nbst = mk.sems["sNB"].n if "sNB" in mk.sems else 0

    def f_ld(sync):
        if nbst:
            mk.wait(sync, "SY", mk.sems["sNB"], nbst)
        sL.inc(sync.dma_start(
            out=mk.sb["smal"][0:1, 40:41],
            in_=mk.dram["NB"].ap()[layer:layer + 1, b:b + 1]), 16)
    blk.sync(f_ld)
    ltok = sL.n

    def f_v(v):
        mk.wait(v, "V", sL, ltok)
        sV.inc(v.tensor_copy(out=mk.sb["rowsb"][0:1, 0:1],
                             in_=mk.sb["smal"][0:1, 40:41]))
    blk.vector(f_v)
    vtok = sV.n

    def f_t(t):
        mk.wait(t, "T", sV, vtok)
        mk.wait(t, "T", sS, sS.n)
        mk.sem("sT").inc(t.matmul(mk.ps[5][:, 0:1],
                                  mk.sb["ones_row"][0:1, 0:D],
                                  mk.sb["rowsb"][0:1, 0:1],
                                  start=True, stop=True))
    blk.tensor(f_t)
    ttok = mk.sems["sT"].n

    def f_s(s):
        mk.wait(s, "S", sT, ttok)
        mk.sem("sS").inc(s.copy(out=mk.sb["negB"][:, layer:layer + 1],
                                in_=mk.ps[5][:, 0:1]))
    blk.scalar(f_s)
    mk.attn_state["negb_tok"] = mk.sems["sS"].n
    mk.attn_state["rb_evac"] = max(mk.attn_state.get("rb_evac", 0),
                                   mk.sems["sS"].n)
    return mk.sems["sS"].n


def emit_build_qe(mk, blk):
    cfg = mk.cfg
    sb = mk.sb
    sV = mk.sem("sV")
    sQE = mk.sem("sQEl")
    sQEst = mk.sem("sQEst")
    units, tot, _ = mk.wmeta["q1"]
    conv = mk.conv_marks.get("q1")
    KC = cfg.KC
    acc0 = 32
    accT = max(KC // 4, 1)
    qeu, qetot = weight_plan(cfg.E, cfg.CKV)
    assert len(qeu) == 1

    acc = ar(mk, acc0, accT)[:, 0:KC * D].rearrange("p (k c) -> p k c", k=KC)
    prev_v = None
    prev_st = 0
    for kv in range(cfg.NKV):
        heads = [cfg.G * kv + j for j in range(cfg.G)]
        uidxs = sorted({h // 4 for h in heads})
        umap = {u: i for i, u in enumerate(uidxs)}

        def f_ld(sync, uidxs=uidxs, kv=kv, prev_v=prev_v):
            if kv == 0:
                if conv:
                    mk.wait(sync, "SY", conv[0], conv[1])
                mk.wait(sync, "SY", sV, mk.sems["sV"].n)
                mk.wait(sync, "SY", mk.sems["sT"], mk.sems["sT"].n)
                mk.wait(sync, "SY", mk.sems["sS"], mk.sems["sS"].n)
                mk.wait(sync, "SY", mk.sems["sP"], mk.sems["sP"].n)
            if prev_v is not None:
                mk.wait(sync, "SY", sV, prev_v)
            for i, ui in enumerate(uidxs):
                u = units[ui]
                sQE.inc(sync.dma_start(
                    out=ar(mk, i * KC, KC)[:, 0:u.nkc * u.ncols],
                    in_=mk.dram["wb_q1"].ap()[:, u.off:u.off + u.nkc * u.ncols]), 16)
        blk.sync(f_ld)
        ltok = sQE.n

        def f_v(v, heads=heads, umap=umap, kv=kv, ltok=ltok, prev_st=prev_st):
            mk.wait(v, "V", sQE, ltok)
            if prev_st:
                mk.wait(v, "V", sQEst, prev_st)
            for j, h in enumerate(heads):
                loc = umap[h // 4]
                coff = (h % 4) * D
                srcap = ar(mk, loc * KC, KC).rearrange(
                    "p (k c) -> p k c", k=KC)[:, :, coff:coff + D]
                mk.wait(v, "V", sV, sV.n)
                if j == 0:
                    sV.inc(v.tensor_copy(out=acc, in_=srcap))
                else:
                    sV.inc(v.tensor_add(acc, acc, srcap))
        blk.vector(f_v)
        prev_v = sV.n

        def f_st(sc, kv=kv, prev_v=prev_v):
            mk.wait(sc, "SC", sV, prev_v)
            dap = mk.dram["wb_qe"].ap()[:, 0:qetot].rearrange(
                "p (k c) -> p k c", k=KC)[:, :, kv * D:(kv + 1) * D]
            sQEst.inc(sc.dma_start(out=dap, in_=acc), 16)
        blk.scalar(f_st)
        prev_st = sQEst.n

    def f_vb(v):
        for kv in range(cfg.NKV):
            dst = sb["rowsb"][0:1, kv * D:(kv + 1) * D]
            for j in range(cfg.G):
                h = cfg.G * kv + j
                srcb = sb["brows"][0:1, h * D:(h + 1) * D]
                mk.wait(v, "V", sV, sV.n)
                if j == 0:
                    sV.inc(v.tensor_copy(out=dst, in_=srcb))
                else:
                    sV.inc(v.tensor_add(dst, dst, srcb))
        mk.wait(v, "V", sV, sV.n)
        sV.inc(v.tensor_copy(out=sb["brows"][0:1, 0:cfg.CKV],
                             in_=sb["rowsb"][0:1, 0:cfg.CKV]))
    blk.vector(f_vb)

    mk.conv_marks["qe"] = (sQEst, sQEst.n)



def emit_qkv(mk, blk, layer, b, tq, xh_toks, xh_tok):
    cfg = mk.cfg
    sT, sV, sS, sP = mk.sem("sT"), mk.sem("sV"), mk.sem("sS"), mk.sem("sP")
    sb = mk.sb
    ws = WStream(mk, blk)
    stg = mk.stg
    kclist = list(range(cfg.KC))
    rhs = lambda kc: sb["xhat"][:, kc * TILE:(kc + 1) * TILE]

    if layer == 0:
        qw, kw, vw = "q0", "k0", "v0"
        qdst, kdst, vdst = "Q0", "K0", "V0"
        nkv_chunks = cfg.KC
        bq_off = bk_off = bv_off = None
        qmqs = list(range(cfg.KC // 4))
    else:
        qw, kw, vw = "qe", "k1", "v1"
        qdst, kdst, vdst = "QE1", "K1", "V1"
        nkv_chunks = cfg.KCV
        bq_off, bk_off = 0, 2 * cfg.E
        bv_off = 2 * cfg.E + cfg.CKV
        qmqs = list(range(cfg.KCV // 4))

    def mk_evac(dst, store_direct, raw_base=16):
        def evac(blk_, mq, psbase, nm, ttok):
            ptok0 = sP.n

            def f_ev(v):
                mk.wait(v, "V", sT, ttok)
                mk.wait(v, "V", sP, ptok0)
                mk.wait(v, "V", sV, sV.n)
                for m in range(nm):
                    h = mq * 4 + m
                    i = v.tensor_copy(out=ar(mk, raw_base + h),
                                      in_=mk.ps[psbase + m][:, :])
                sV.inc(i)
            blk_.vector(f_ev)
            ev_tok = sV.n
            for m in range(nm):
                h = mq * 4 + m
                raw = ar(mk, raw_base + h)
                rtok = emit_rope(mk, blk_, sV, ev_tok, raw, tq,
                                 4 + (h % 2), h % 4, eng="P")
                if store_direct:
                    s = stg.next()

                    def f_ev2(v, s=s, rtok=rtok, h=h):
                        mk.wait(v, "V", sT, rtok)
                        stg.wait_free(v, "V", s)
                        mk.wait(v, "V", sV, sV.n)
                        sV.inc(v.tensor_copy(out=stg.slot_ap(s),
                                             in_=mk.ps[4 + (h % 2)][:, :]))
                    blk_.vector(f_ev2)
                    mk._rope_ps_free[4 + (h % 2)] = (sV, sV.n)
                    dap = mk.dram[dst].ap()[b, h, :, tq * TILE:(tq + 1) * TILE]
                    stg.store(s, dap, sV, sV.n)
                else:
                    def f_ev2(v, rtok=rtok, h=h):
                        mk.wait(v, "V", sT, rtok)
                        mk.wait(v, "V", sV, sV.n)
                        sV.inc(v.tensor_copy(out=ar(mk, 0 + h),
                                             in_=mk.ps[4 + (h % 2)][:, :]))
                    blk_.vector(f_ev2)
                    mk._rope_ps_free[4 + (h % 2)] = (sV, sV.n)
            return (sV, ev_tok)
        return evac

    fm_proj(mk, blk, ws, qw, 0, rhs, sV, xh_tok, kclist,
            qmqs, mk_evac(qdst, True),
            bias_row_off=bq_off, psmode="quad", rhs_toks=xh_toks)
    kraw = 0 if layer == 0 else 0

    fm_proj(mk, blk, ws, kw, 0, rhs, sV, xh_tok, kclist,
            list(range(nkv_chunks // 4)), mk_evac(kdst, True, raw_base=kraw),
            bias_row_off=bk_off, psmode="quad")
    if layer == 0:
        emit_norms(mk, blk, col=0, src_base=16, nch=cfg.KC)
        emit_norms(mk, blk, col=1, src_base=0, nch=cfg.KC)
    else:
        emit_norms(mk, blk, col=2, src_base=16, nch=cfg.KCV)
        emit_norms(mk, blk, col=3, src_base=kraw, nch=nkv_chunks)

    units, tot, _ = mk.wmeta[vw]
    vbar = (sV, sV.n)
    psfree = [vbar, vbar]
    for oq in range(nkv_chunks // 4):
        u = units[oq]
        ld = ws.load(vw, 0, u)
        for ts in range(4):
            pidx = ts % 2
            pf = psfree[pidx]

            def f_t(t, ld=ld, ts=ts, pidx=pidx, pf=pf, u=u):
                mk.wait(t, "T", sV, xh_tok)
                mk.wait(t, "T", ld.sem, ld.tok)
                if pf is not None:
                    mk.wait(t, "T", pf[0], pf[1])
                last = None
                bias = bv_off is not None
                for ki in range(u.nkc):
                    last = t.matmul(
                        mk.ps[pidx][:, 0:u.ncols],
                        sb["xhat"][:, ki * TILE + ts * D:ki * TILE + (ts + 1) * D],
                        ld.buf[:, ki * u.ncols:(ki + 1) * u.ncols],
                        start=(ki == 0), stop=(ki == u.nkc - 1 and not bias))
                if bias:
                    last = t.matmul(
                        mk.ps[pidx][:, 0:u.ncols],
                        sb["ones_row"][0:1, 0:D],
                        sb["brows"][0:1, bv_off + u.col0:bv_off + u.col0 + u.ncols],
                        start=False, stop=True)
                sT.inc(last)
            blk.tensor(f_t)
            ttok = sT.n
            s = stg.next()

            def f_ev(v, ttok=ttok, pidx=pidx, s=s, u=u):
                mk.wait(v, "V", sT, ttok)
                stg.wait_free(v, "V", s)
                sV.inc(v.tensor_copy(out=stg.slot_ap(s)[:, 0:u.ncols],
                                     in_=mk.ps[pidx][:, 0:u.ncols]))
            blk.vector(f_ev)
            psfree[pidx] = (sV, sV.n)
            vt = sV.n
            nslice = u.ncols // D

            def f_st(sc, s=s, ts=ts, oq=oq, vt=vt, nslice=nslice):
                mk.wait(sc, "SC", sV, vt)
                for j in range(nslice):
                    h = oq * 4 + j
                    dap = mk.dram[vdst].ap()[
                        b, h, tq * TILE + ts * D:tq * TILE + (ts + 1) * D, :]
                    stg.sems[s].inc(
                        sc.dma_start(out=dap,
                                     in_=stg.slot_ap(s)[:, j * D:(j + 1) * D]), 16)
            blk.scalar(f_st)
            stg.uses[s] += nslice - 1
            stg.uses[s] += 1
        ws.mark_consumed(ld.slot, sT, sT.n)


def emit_norms_strided(mk, blk, col, idxs):
    cfg = mk.cfg
    sT, sV, sP = mk.sem("sT"), mk.sem("sV"), mk.sem("sP")
    nE = cfg.KCV
    Emat = mk.sb["E4"]
    ptok0 = sP.n
    tbase = sT.n
    vtok0 = sV.n
    stok0 = mk.sems["sS"].n

    def f_p(p):
        mk.wait(p, "P", sV, vtok0)
        mk.wait(p, "P", mk.sems["sS"], stok0)
        for i, idx in enumerate(idxs):
            if i >= 2:
                mk.wait(p, "P", sT, tbase + (i - 1), future=True)
            src = ar(mk, idx)
            sP.inc(p.tensor_tensor(ar(mk, 62 + (i % 2)), src, src, ALU.mult))
    blk.gpsimd(f_p)

    vcur0 = sV.n

    def f_t(t):
        mk.wait(t, "T", sV, vcur0)
        for i in range(len(idxs)):
            mk.wait(t, "T", sP, ptok0 + i + 1)
            sT.inc(t.matmul(mk.ps[2][0:nE, :],
                            Emat[:, i * nE:(i + 1) * nE],
                            ar(mk, 62 + (i % 2)),
                            start=(i == 0), stop=(i == len(idxs) - 1)))
    blk.tensor(f_t)
    ttok = sT.n

    def f_v(v):
        mk.wait(v, "V", sT, ttok)
        sV.inc(v.reduce_max(mk.sb["smal"][0:nE, 32:33], mk.ps[2][0:nE, :],
                            axis=AXX))
        mk.wait(v, "V", sV, sV.n)
        sV.inc(v.tensor_tensor(mk.sb["nrm"][0:nE, col:col + 1],
                               mk.sb["nrm"][0:nE, col:col + 1],
                               mk.sb["smal"][0:nE, 32:33], ALU.max))
    blk.vector(f_v)



def emit_attn2(mk, blk, layer, b, h0):
    cfg = mk.cfg
    sT, sV, sS, sP = mk.sem("sT"), mk.sem("sV"), mk.sem("sS"), mk.sem("sP")
    sb = mk.sb
    qsrc = "Q0" if layer == 0 else "QE1"
    ksrc = "K0" if layer == 0 else "K1"
    vsrc = "V0" if layer == 0 else "V1"
    odst = "O0" if layer == 0 else "O1"
    st = mk.attn_state
    stride = 3 * cfg.TQ
    bufs = []
    for par in (0, 1):
        qb = ar(mk, 40 + par * stride, cfg.TQ)
        kb = ar(mk, 40 + par * stride + cfg.TQ, cfg.TQ)
        vb = ar(mk, 40 + par * stride + 2 * cfg.TQ, cfg.TQ)
        bufs.append((qb, kb, vb))
    sQL = mk.sem("sQL0")
    sOst = [mk.sem(f"sOst{i}") for i in range(8)]
    prev_cons = st.get("cons")
    negB = sb["negB"][:, layer:layer + 1]
    negb_tok = st.get("negb_tok", 0)
    daccB = sb["fstage"]

    def f_ld(sync):
        if prev_cons is not None:
            mk.wait(sync, "SY", prev_cons[0], prev_cons[1], future=True)
        for x, h in ((0, h0), (1, h0 + 1)):
            qb, kb, vb = bufs[x]
            sQL.inc(sync.dma_start(out=qb, in_=mk.dram[qsrc].ap()[b, h]), 16)
            sQL.inc(sync.dma_start(out=kb, in_=mk.dram[ksrc].ap()[b, h]), 16)
            sQL.inc(sync.dma_start(
                out=vb.rearrange("p (k d) -> p k d", d=D),
                in_=mk.dram[vsrc].ap()[b, h].rearrange("(k p) d -> p k d", p=D)), 16)
    blk.sync(f_ld)
    ldtok = sQL.n

    groups = list(range(cfg.TQ))
    PT0 = {0: 0, 1: 10}
    psO_ = {0: mk.ps[3], 1: mk.ps[4]}

    def ptpair(x, k):
        return ar(mk, PT0[x] + 2 * (k % 2), 2)

    def ptc(x, c):
        return ar(mk, PT0[x] + (c % 4))

    nT, nS, nV = sT.n, sS.n, sV.n
    smm, expp, pv, dnm = {}, {}, {}, {}
    maskt, accB = {}, {}
    dmmA, dmmB, lnd, expd, rbmm, rbcopy, fin = {}, {}, {}, {}, {}, {}, {}
    for g in groups:
        nk = 4 * (g + 1)
        npair = nk // 2
        for x in (0, 1):
            nT += 2; smm[(x, g, 0)] = nT
        if g > 0:
            for x in (0, 1):
                nT += 1; rbmm[(x, g - 1)] = nT
        for k in range(npair):
            for x in (0, 1):
                if k + 1 < npair:
                    nT += 2; smm[(x, g, k + 1)] = nT
                for c in (2 * k, 2 * k + 1):
                    nT += 1; pv[(x, g, c)] = nT
                    if x == 0:
                        nT += 1; dnm[(g, c)] = nT
        dmmA[g] = dnm[(g, nk - 1)]
        nT += 1; dmmB[g] = nT
        for k in range(npair):
            for x in (0, 1):
                nS += 1; expp[(x, g, k)] = nS
        for x in (0, 1):
            nS += 1; lnd[(x, g)] = nS
            nS += 1; expd[(x, g)] = nS
        for k in range(npair):
            for c in (2 * k, 2 * k + 1):
                for x in (0, 1):
                    if c >= nk - 4:
                        nV += 1; maskt[(x, g, c)] = nV
                    if x == 1:
                        nV += 1; accB[(g, c)] = nV
        nV += 1
        nV -= 1
        for x in (0, 1):
            nV += 1; rbcopy[(x, g)] = nV
            nV += 1; fin[(x, g)] = nV
    gl = groups[-1]
    for x in (0, 1):
        nT += 1; rbmm[(x, gl)] = nT

    pv_prev_exp = st.get("exp_last", 0)
    pv_prev_maskA = st.get("maskA_last", 0)
    pv_prev_maskB = st.get("maskB_last", 0)
    prev_accB = st.get("accB_last", 0)
    prev_dmmB = st.get("dmmB_last", 0)
    prev_lndB = st.get("lndB_last", 0)
    prev_rbcopyB = st.get("rbcopyB_last", 0)
    prev_rbmmB = st.get("rbmmB_last", 0)
    prev_finA = st.get("finA_last")
    prev_finB = st.get("finB_last")
    ost_uses = [st.get(f"ost_uses{i}", 0) for i in range(8)]

    rrows = [sb["stats"][0:1, TILE:2 * TILE], sb["stats"][0:1, 2 * TILE:3 * TILE]]
    rbfs = [sb["rowsb"][0:1, 0:TILE], sb["rowsb"][0:1, TILE:2 * TILE]]

    def f_t(t):
        for g in groups:
            nk = 4 * (g + 1)
            npair = nk // 2
            qs = slice(g * TILE, (g + 1) * TILE)

            def wait_pair(x, k):
                if k >= 1:
                    mk.wait(t, "T", sS, expp[(x, g, k - 1)], future=True)
                elif g > 0:
                    mk.wait(t, "T", sS, expp[(x, g - 1, 2 * g - 1)], future=True)
                elif pv_prev_exp:
                    mk.wait(t, "T", sS, pv_prev_exp, future=True)

            mk.wait(t, "T", sQL, ldtok)
            for x in (0, 1):
                qb, kb, vb = bufs[x]
                wait_pair(x, 0)
                pair = mk.psp[x]
                t.matmul(pair[:, 0:TILE], kb[:, 0:D], qb[:, qs],
                         start=True, stop=True)
                sT.inc(t.matmul(pair[:, TILE:2 * TILE], kb[:, D:2 * D],
                                qb[:, qs], start=True, stop=True), 2)
                assert sT.n == smm[(x, g, 0)]
            if g > 0:
                for x in (0, 1):
                    mk.wait(t, "T", sS, expd[(x, g - 1)], future=True)
                    if x == 1:
                        mk.wait(t, "T", sV, rbcopy[(0, g - 1)], future=True)
                    elif g > 1:
                        mk.wait(t, "T", sV, rbcopy[(1, g - 2)], future=True)
                    elif prev_rbcopyB:
                        mk.wait(t, "T", sV, prev_rbcopyB, future=True)
                    sT.inc(t.matmul(mk.ps[5][:, :], sb["ones_row"][0:1, 0:D],
                                    rbfs[x], start=True, stop=True))
                    assert sT.n == rbmm[(x, g - 1)]
            for k in range(npair):
                for x in (0, 1):
                    qb, kb, vb = bufs[x]
                    if k + 1 < npair:
                        wait_pair(x, k + 1)
                        pair = mk.psp[x]
                        c2 = 2 * (k + 1)
                        t.matmul(pair[:, 0:TILE], kb[:, c2 * D:(c2 + 1) * D],
                                 qb[:, qs], start=True, stop=True)
                        sT.inc(t.matmul(pair[:, TILE:2 * TILE],
                                        kb[:, (c2 + 1) * D:(c2 + 2) * D],
                                        qb[:, qs], start=True, stop=True), 2)
                        assert sT.n == smm[(x, g, k + 1)]
                    for c in (2 * k, 2 * k + 1):
                        if c >= nk - 4:
                            mk.wait(t, "T", sV, maskt[(x, g, c)], future=True)
                        else:
                            mk.wait(t, "T", sS, expp[(x, g, k)], future=True)
                        if c == 0:
                            fp = (fin[(x, g - 1)] if g > 0 else None)
                            if fp is not None:
                                mk.wait(t, "T", sV, fp, future=True)
                            elif x == 0 and prev_finA is not None:
                                mk.wait(t, "T", prev_finA[0], prev_finA[1],
                                        future=True)
                            elif x == 1 and prev_finB is not None:
                                mk.wait(t, "T", prev_finB[0], prev_finB[1],
                                        future=True)
                            if x == 0:
                                if g > 0:
                                    mk.wait(t, "T", sS, lnd[(1, g - 1)],
                                            future=True)
                                elif prev_lndB:
                                    mk.wait(t, "T", sS, prev_lndB, future=True)
                        sT.inc(t.matmul(psO_[x][:, :], vb[:, c * D:(c + 1) * D],
                                        ptc(x, c), start=(c == 0),
                                        stop=(c == nk - 1)))
                        assert sT.n == pv[(x, g, c)]
                        if x == 0:
                            sT.inc(t.matmul(mk.ps[2][0:1, :], sb["ones_col"][:],
                                            ptc(0, c), start=(c == 0),
                                            stop=(c == nk - 1)))
                            assert sT.n == dnm[(g, c)]
            mk.wait(t, "T", sV, accB[(g, nk - 1)], future=True)
            mk.wait(t, "T", sS, lnd[(0, g)], future=True)
            sT.inc(t.matmul(mk.ps[2][0:1, :], sb["onesf"][:], daccB[:],
                            start=True, stop=True))
            assert sT.n == dmmB[g]
        for x in (0, 1):
            mk.wait(t, "T", sS, expd[(x, gl)], future=True)
            if x == 1:
                mk.wait(t, "T", sV, rbcopy[(0, gl)], future=True)
            elif gl > 0:
                mk.wait(t, "T", sV, rbcopy[(1, gl - 1)], future=True)
            elif prev_rbcopyB:
                mk.wait(t, "T", sV, prev_rbcopyB, future=True)
            sT.inc(t.matmul(mk.ps[5][:, :], sb["ones_row"][0:1, 0:D],
                            rbfs[x], start=True, stop=True))
            assert sT.n == rbmm[(x, gl)]
    blk.tensor(f_t)
    assert sT.n == nT

    def f_s(s):
        first = True
        for g in groups:
            nk = 4 * (g + 1)
            npair = nk // 2
            for k in range(npair):
                for x in (0, 1):
                    mk.wait(s, "S", sT, smm[(x, g, k)])
                    if first and negb_tok:
                        mk.wait(s, "S", sS, negb_tok)
                        first = False
                    kk = k - 2
                    if kk >= 0:
                        if x == 0:
                            mk.wait(s, "S", sT, dnm[(g, 2 * kk + 1)])
                        else:
                            mk.wait(s, "S", sV, accB[(g, 2 * kk + 1)],
                                    future=True)
                        if (x, g, 2 * kk + 1) in maskt:
                            mk.wait(s, "S", sV, maskt[(x, g, 2 * kk + 1)],
                                    future=True)
                    elif g > 0:
                        cl = 4 * g - 1
                        if x == 0:
                            mk.wait(s, "S", sT, dnm[(g - 1, cl)])
                        else:
                            mk.wait(s, "S", sV, accB[(g - 1, cl)], future=True)
                        mk.wait(s, "S", sV, maskt[(x, g - 1, cl)], future=True)
                    else:
                        if x == 0 and pv_prev_maskA:
                            mk.wait(s, "S", sV, pv_prev_maskA)
                        if x == 1 and pv_prev_maskB:
                            mk.wait(s, "S", sV, pv_prev_maskB)
                        if x == 1 and prev_accB:
                            mk.wait(s, "S", sV, prev_accB)
                    sS.inc(s.activation(out=ptpair(x, k), in_=mk.psp[x][:, :],
                                        func=AF.Exp, bias=negB,
                                        scale=cfg.scale))
                    assert sS.n == expp[(x, g, k)]
            mk.wait(s, "S", sT, dmmA[g])
            sS.inc(s.activation(out=rrows[0], in_=mk.ps[2][0:1, :], func=AF.Ln))
            assert sS.n == lnd[(0, g)]
            mk.wait(s, "S", sS, sS.n)
            mk.wait(s, "S", sT, rbmm[(0, g - 1)] if g > 0 else
                    st.get("rbmmA_last", 0))
            sS.inc(s.activation(out=rbfs[0], in_=rrows[0], func=AF.Exp,
                                scale=-1.0))
            assert sS.n == expd[(0, g)]
            mk.wait(s, "S", sT, dmmB[g])
            sS.inc(s.activation(out=rrows[1], in_=mk.ps[2][0:1, :], func=AF.Ln))
            assert sS.n == lnd[(1, g)]
            mk.wait(s, "S", sS, sS.n)
            mk.wait(s, "S", sT, rbmm[(1, g - 1)] if g > 0 else prev_rbmmB)
            sS.inc(s.activation(out=rbfs[1], in_=rrows[1], func=AF.Exp,
                                scale=-1.0))
            assert sS.n == expd[(1, g)]
    blk.scalar(f_s)
    assert sS.n == nS

    def f_v(v):
        for g in groups:
            nk = 4 * (g + 1)
            npair = nk // 2
            for k in range(npair):
                for c in (2 * k, 2 * k + 1):
                    for x in (0, 1):
                        pt = ptc(x, c)
                        mk.wait(v, "V", sS, expp[(x, g, k)])
                        if c >= nk - 4:
                            j = c - (nk - 4)
                            mk.wait(v, "V", sV, sV.n)
                            sV.inc(v.tensor_tensor(
                                pt, pt, sb["masks"][:, j * TILE:(j + 1) * TILE],
                                ALU.mult))
                            assert sV.n == maskt[(x, g, c)]
                        if x == 1:
                            mk.wait(v, "V", sV, sV.n)
                            if c == 0:
                                mk.wait(v, "V", sT,
                                        dmmB[g - 1] if g > 0 else prev_dmmB)
                                sV.inc(v.tensor_copy(out=daccB[:], in_=pt))
                            else:
                                sV.inc(v.tensor_tensor(daccB[:], daccB[:], pt,
                                                       ALU.add))
                            assert sV.n == accB[(g, c)]
            for x in (0, 1):
                osl = 4 * x + g
                mk.wait(v, "V", sT, rbmm[(x, g)])
                mk.wait(v, "V", sV, sV.n)
                sV.inc(v.tensor_copy(out=ar(mk, 14 + x), in_=mk.ps[5][:, :]))
                assert sV.n == rbcopy[(x, g)]
                if ost_uses[osl] > 0:
                    mk.wait(v, "V", sOst[osl], 16 * ost_uses[osl], future=True)
                mk.wait(v, "V", sV, sV.n)
                sV.inc(v.tensor_tensor(ar(mk, 16 + osl), psO_[x][:, :],
                                       ar(mk, 14 + x), ALU.mult))
                assert sV.n == fin[(x, g)]
                ost_uses[osl] += 1
    blk.vector(f_v)
    assert sV.n == nV

    def f_st(sc):
        for g in groups:
            for x in (0, 1):
                osl = 4 * x + g
                mk.wait(sc, "SC", sV, fin[(x, g)])
                dap = mk.dram[odst].ap()[b, h0 + x, :,
                                         g * TILE:(g + 1) * TILE]
                sOst[osl].inc(sc.dma_start(out=dap, in_=ar(mk, 16 + osl)), 16)
    blk.scalar(f_st)

    st["cons"] = (sT, sT.n)
    nkl = 4 * cfg.TQ
    st["exp_last"] = expp[(1, gl, nkl // 2 - 1)]
    st["maskA_last"] = maskt[(0, gl, nkl - 1)]
    st["maskB_last"] = maskt[(1, gl, nkl - 1)]
    st["accB_last"] = accB[(gl, nkl - 1)]
    st["dmmB_last"] = dmmB[gl]
    st["lndB_last"] = lnd[(1, gl)]
    st["rbcopyB_last"] = rbcopy[(1, gl)]
    st["rbmmA_last"] = rbmm[(0, gl)]
    st["rbmmB_last"] = rbmm[(1, gl)]
    st["finA_last"] = (sV, fin[(0, gl)])
    st["finB_last"] = (sV, fin[(1, gl)])
    for i in range(8):
        st[f"ost_uses{i}"] = ost_uses[i]



def emit_p3_tile(mk, blk, b, tq):
    cfg = mk.cfg
    sT, sV, sS = mk.sem("sT"), mk.sem("sV"), mk.sem("sS")
    sb = mk.sb
    ws = WStream(mk, blk)
    sXL = mk.sem("sXL")
    sOL = mk.sem("sOL")
    obf = ar(mk, 0, cfg.KC)

    prevu = getattr(mk, "p3_prev_user", None)
    if prevu is None:
        prevu = [(sV, sV.n), (sT, sT.n), (sS, sS.n)]

    def f_ldo(sc):
        for sm_, tk_ in prevu:
            mk.wait(sc, "SC", sm_, tk_)
        apo = mk.dram["O0"].ap()[b, :, :, tq * TILE:(tq + 1) * TILE]
        sOL.inc(sc.dma_start(
            out=obf.rearrange("p (k t) -> p k t", k=cfg.KC),
            in_=apo.rearrange("k p t -> p k t")), 16)
    blk.scalar(f_ldo)
    otok = sOL.n

    vtok_x = sV.n

    def f_ldx(sync):
        mk.wait(sync, "SY", sV, vtok_x)
        for nm_ in ("sX2st",):
            if nm_ in mk.sems:
                mk.wait(sync, "SY", mk.sems[nm_], mk.sems[nm_].n)
        sXL.inc(sync.dma_start(
            out=sb["Xres"][:].rearrange("p (k t) -> p k t", k=cfg.KC),
            in_=mk.dram["Xin"].ap()[b, tq]), 16)
    blk.sync(f_ldx)
    xltok = sXL.n

    def rhs(kc):
        return obf[:, kc * TILE:(kc + 1) * TILE]

    def evac(blk_, mq, psbase, nm, ttok_):
        def f_v(v):
            mk.wait(v, "V", sT, ttok_)
            mk.wait(v, "V", sXL, xltok)
            for m in range(nm):
                kc = mq * 4 + m
                x = sb["Xres"][:, kc * TILE:(kc + 1) * TILE]
                i = v.tensor_tensor(x, mk.ps[psbase + m][:, :], x, ALU.add)
            sV.inc(i)
        blk_.vector(f_v)
        return (sV, sV.n)

    fm_proj(mk, blk, ws, "o0", 0, rhs, sOL, otok,
            list(range(cfg.KC)), list(range(cfg.KC // 4)), evac)
    x1_tok = sV.n

    def f_cast(v):
        mk.wait(v, "V", sV, x1_tok)
        mk.wait(v, "V", sT, sT.n)
        for kc in range(cfg.KC):
            i = v.tensor_copy(out=obf[:, kc * TILE:(kc + 1) * TILE],
                              in_=sb["Xres"][:, kc * TILE:(kc + 1) * TILE])
        sV.inc(i)
    blk.vector(f_cast)

    xh_toks, xh_tok = emit_ln(mk, blk, b, tq, None, xbf_off=0, sq_off=16,
                              xbf_ready=(sV, sV.n))

    units, tot, _ = mk.wmeta["gu0"]
    ngq = cfg.DFF // TILE
    prev_gate = None
    prev_up = None
    for mq in range(ngq):
        ldg = ws.load("gu0", 0, units[mq])

        def f_tg(t, ldg=ldg, pf=prev_gate, mq=mq):
            if mq > 0:
                mk.wait(t, "T", sV, xh_tok)
            mk.wait(t, "T", ldg.sem, ldg.tok)
            if pf is not None:
                mk.wait(t, "T", pf[0], pf[1])
            last = None
            for ki in range(cfg.KC):
                if mq == 0 and ki % 4 == 0:
                    mk.wait(t, "T", sV, xh_toks[min(ki + 3, cfg.KC - 1)])
                for m in range(4):
                    last = t.matmul(
                        mk.ps[m][:, :],
                        ldg.buf[:, ki * TILE + m * D:ki * TILE + (m + 1) * D],
                        sb["xhat"][:, ki * TILE:(ki + 1) * TILE],
                        start=(ki == 0), stop=(ki == cfg.KC - 1))
            sT.inc(last)
        blk.tensor(f_tg)
        gate_tok = sT.n
        ws.mark_consumed(ldg.slot, sT, gate_tok)
        ldu = ws.load("gu0", 0, units[ngq + mq])

        def f_tu(t, ldu=ldu, pf=prev_up):
            mk.wait(t, "T", ldu.sem, ldu.tok)
            if pf is not None:
                mk.wait(t, "T", pf[0], pf[1])
            last = None
            for ki in range(cfg.KC):
                for m in range(4):
                    last = t.matmul(
                        mk.ps[4 + m][:, :],
                        ldu.buf[:, ki * TILE + m * D:ki * TILE + (m + 1) * D],
                        sb["xhat"][:, ki * TILE:(ki + 1) * TILE],
                        start=(ki == 0), stop=(ki == cfg.KC - 1))
            sT.inc(last)
        blk.tensor(f_tu)
        up_tok = sT.n
        ws.mark_consumed(ldu.slot, sT, up_tok)

        def f_si(s, gate_tok=gate_tok, pu=prev_up):
            mk.wait(s, "S", sT, gate_tok)
            if pu is not None:
                mk.wait(s, "S", pu[0], pu[1])
            for m in range(4):
                i = s.activation(out=ar(mk, 64 + m), in_=mk.ps[m][:, :],
                                 func=AF.Silu)
            sS.inc(i)
        blk.scalar(f_si)
        si_tok = sS.n
        prev_gate = (sS, si_tok)

        def f_h(v, si_tok=si_tok, up_tok=up_tok, mq=mq):
            mk.wait(v, "V", sS, si_tok)
            mk.wait(v, "V", sT, up_tok)
            for m in range(4):
                fc = mq * 4 + m
                i = v.tensor_tensor(ar(mk, fc), ar(mk, 64 + m),
                                    mk.ps[4 + m][:, :], ALU.mult)
            sV.inc(i)
        blk.vector(f_h)
        prev_up = (sV, sV.n)
    h_tok = sV.n

    units_d, _, _ = mk.wmeta["dn0"]
    nu = (cfg.KFF + 15) // 16
    prev_free = [None, None]
    nmq_d = cfg.KC // 4
    for mq in range(nmq_d):
        psbase = 4 * (mq % 2)
        pf = prev_free[mq % 2]
        for kq in range(nu):
            u = units_d[mq * nu + kq]
            ld = ws.load("dn0", 0, u)

            def f_t(t, ld=ld, u=u, kq=kq, psbase=psbase, pf=pf):
                mk.wait(t, "T", sV, h_tok)
                mk.wait(t, "T", ld.sem, ld.tok)
                if kq == 0 and pf is not None:
                    mk.wait(t, "T", pf[0], pf[1])
                last = None
                for ki in range(u.nkc):
                    fc = u.kc0 + ki
                    for m in range(4):
                        last = t.matmul(
                            mk.ps[psbase + m][:, :],
                            ld.buf[:, ki * TILE + m * D:ki * TILE + (m + 1) * D],
                            ar(mk, fc),
                            start=(kq == 0 and ki == 0),
                            stop=(kq == nu - 1 and ki == u.nkc - 1))
                sT.inc(last)
            blk.tensor(f_t)
            ws.mark_consumed(ld.slot, sT, sT.n)
            if mq == nmq_d - 1 and kq == 0:
                mk.p3_prev_user = [(sT, sT.n)]
        ttok_ = sT.n

        def f_v(v, mq=mq, psbase=psbase, ttok_=ttok_):
            mk.wait(v, "V", sT, ttok_)
            for m in range(4):
                kc = mq * 4 + m
                x = sb["Xres"][:, kc * TILE:(kc + 1) * TILE]
                i = v.tensor_tensor(x, mk.ps[psbase + m][:, :], x, ALU.add)
            sV.inc(i)
        blk.vector(f_v)
        prev_free[mq % 2] = (sV, sV.n)
    x2_tok = sV.n

    def f_st(sc):
        mk.wait(sc, "SC", sV, x2_tok)
        mk.sem("sX2st").inc(sc.dma_start(
            out=mk.dram["X2"].ap()[b, tq],
            in_=sb["Xres"][:].rearrange("p (k t) -> p k t", k=cfg.KC)), 16)
    blk.scalar(f_st)



def emit_p6_tile(mk, blk, b, tq):
    cfg = mk.cfg
    sT, sV, sS = mk.sem("sT"), mk.sem("sV"), mk.sem("sS")
    sb = mk.sb
    sm = sb["smal"]
    ws = WStream(mk, blk)
    sXL = mk.sem("sXL")
    sOL = mk.sem("sOL")
    obf = ar(mk, 0, cfg.KCV)

    prevu = getattr(mk, "p6_prev_user", None)
    if prevu is None:
        prevu = [(sV, sV.n), (sT, sT.n), (sS, sS.n)]

    def f_ldo(sc):
        for sm_, tk_ in prevu:
            mk.wait(sc, "SC", sm_, tk_)
        apo = mk.dram["O1"].ap()[b, :, :, tq * TILE:(tq + 1) * TILE]
        sOL.inc(sc.dma_start(
            out=obf.rearrange("p (k t) -> p k t", k=cfg.KCV),
            in_=apo.rearrange("k p t -> p k t")), 16)
    blk.scalar(f_ldo)
    otok = sOL.n

    vtok_x = sV.n

    def f_ldx(sync):
        mk.wait(sync, "SY", sV, vtok_x)
        for nm_ in ("sOutSt", "sX3st"):
            if nm_ in mk.sems:
                mk.wait(sync, "SY", mk.sems[nm_], mk.sems[nm_].n)
        sXL.inc(sync.dma_start(
            out=sb["Xres"][:].rearrange("p (k t) -> p k t", k=cfg.KC),
            in_=mk.dram["X2"].ap()[b, tq]), 16)
    blk.sync(f_ldx)
    xltok = sXL.n

    def rhs(kc):
        return obf[:, kc * TILE:(kc + 1) * TILE]

    def evac(blk_, mq, psbase, nm, ttok_):
        def f_v(v):
            mk.wait(v, "V", sT, ttok_)
            mk.wait(v, "V", sXL, xltok)
            for m in range(nm):
                kc = mq * 4 + m
                x = sb["Xres"][:, kc * TILE:(kc + 1) * TILE]
                i = v.tensor_tensor(x, mk.ps[psbase + m][:, :], x, ALU.add)
            sV.inc(i)
        blk_.vector(f_v)
        return (sV, sV.n)

    fm_proj(mk, blk, ws, "o1", 0, rhs, sOL, otok,
            list(range(cfg.KCV)), list(range(cfg.KC // 4)), evac,
            bias_row_off=cfg.E)
    x3_tok = sV.n

    def f_stx3(sc):
        mk.wait(sc, "SC", sV, x3_tok)
        mk.sem("sX3st").inc(sc.dma_start(
            out=mk.dram["X3"].ap()[b, tq],
            in_=sb["Xres"][:].rearrange("p (k t) -> p k t", k=cfg.KC)), 16)
    blk.scalar(f_stx3)
    x3st = mk.sems["sX3st"].n

    def f_cast(v):
        mk.wait(v, "V", sV, x3_tok)
        mk.wait(v, "V", sT, sT.n)
        for kc in range(cfg.KC):
            i = v.tensor_copy(out=ar(mk, kc),
                              in_=sb["Xres"][:, kc * TILE:(kc + 1) * TILE])
        sV.inc(i)
    blk.vector(f_cast)

    xh_toks, xh_tok = emit_ln(mk, blk, b, tq, None, xbf_off=0, sq_off=16,
                              xbf_ready=(sV, sV.n))

    def f_tg(t):
        mk.wait(t, "T", sXL, xltok)
        mk.wait(t, "T", sV, xh_tok)
        for kc in range(cfg.KC):
            i = t.matmul(mk.ps[6][0:cfg.NE, :],
                         sb["wg"][:, kc * cfg.NE:(kc + 1) * cfg.NE],
                         sb["Xres"][:, kc * TILE:(kc + 1) * TILE],
                         start=(kc == 0), stop=False)
        sT.inc(i)
    blk.tensor(f_tg)

    def f_vn(v):
        mk.wait(v, "V", sV, xh_tok)
        sV.inc(v.tensor_scalar_mul(sm[0:1, 40:40 + cfg.NE], sb["csum"][:], -1.0))
        sV.inc(v.memset(sm[0:1, 48:48 + cfg.NE], 1.0))
    blk.vector(f_vn)
    vtok_n = sV.n

    def f_tg2(t):
        mk.wait(t, "T", sV, vtok_n)
        sT.inc(t.matmul(mk.ps[6][0:cfg.NE, :], sm[0:1, 40:40 + cfg.NE],
                        sb["stats"][0:1, 0:TILE], start=False, stop=True))
        sT.inc(t.matmul(mk.ps[5][0:cfg.NE, :], sm[0:1, 48:48 + cfg.NE],
                        sb["stats"][0:1, 3 * TILE:4 * TILE],
                        start=True, stop=True))
    blk.tensor(f_tg2)
    tg_tok = sT.n

    def f_vg(v):
        mk.wait(v, "V", sT, tg_tok)
        sV.inc(v.tensor_copy(out=sb["g8"][:, :], in_=mk.ps[5][0:cfg.NE, :]))
        mk.wait(v, "V", sV, sV.n)
        sV.inc(v.tensor_tensor(sb["g8s"][:, :], mk.ps[6][0:cfg.NE, :],
                               sb["g8"][:, :], ALU.mult))
    blk.vector(f_vg)
    g_tok = sV.n

    sm_st = lambda ts: sm[:, ts * 8:(ts + 1) * 8]
    sm_s2 = lambda ts: sm[:, 32 + ts * 8:40 + ts * 8]
    sm_m1 = lambda ts: sm[:, 64 + ts:65 + ts]
    sm_w = lambda ts: sm[:, 96 + ts * 8:104 + ts * 8]
    mx = sm[:, 68:76]
    din = sm[:, 88:92]
    dex = sm[:, 92:96]
    for ts in range(4):
        def f_tt(t, ts=ts):
            mk.wait(t, "T", sV, g_tok if ts == 0 else sV.n)
            sT.inc(t.transpose(mk.ps[5][0:D, 0:cfg.NE],
                               sb["g8s"][:, ts * D:(ts + 1) * D],
                               sb["ident_f"][0:cfg.NE, 0:cfg.NE]))
        blk.tensor(f_tt)
        tt_tok = sT.n

        def f_vt(v, tt_tok=tt_tok, ts=ts):
            mk.wait(v, "V", sT, tt_tok)
            mk.wait(v, "V", sV, sV.n)
            sV.inc(v.tensor_copy(out=sm_st(ts), in_=mk.ps[5][0:D, 0:cfg.NE]))
            mk.wait(v, "V", sV, sV.n)
            sV.inc(v.max(mx, sm_st(ts)))
            mk.wait(v, "V", sV, sV.n)
            sV.inc(v.tensor_copy(out=sm_m1(ts), in_=mx[:, 0:1]))
            sV.inc(v.memset(mx[:, 2:8], -1e30))
            mk.wait(v, "V", sV, sV.n)
            sV.inc(v.match_replace(sm_s2(ts), mx, sm_st(ts), -1e30))
            sV.inc(v.tensor_tensor(din[:, ts:ts + 1], mx[:, 1:2], mx[:, 0:1],
                                   ALU.subtract))
        blk.vector(f_vt)
    v1 = sV.n

    def f_se(s, v1=v1):
        mk.wait(s, "S", sV, v1)
        sS.inc(s.activation(out=dex, in_=din, func=AF.Exp))
    blk.scalar(f_se)
    s1 = sS.n

    for ts in range(4):
        def f_vw(v, s1=s1, ts=ts):
            mk.wait(v, "V", sS, s1)
            e1c = dex[:, ts:ts + 1]
            c2 = sm[:, 76:77]
            p1 = sm[:, 77:78]
            p2 = sm[:, 78:79]
            pd = sm[:, 79:80]
            dm = sm[:, 80:88]
            mk.wait(v, "V", sV, sV.n)
            sV.inc(v.tensor_scalar_add(c2, e1c, 1.0))
            mk.wait(v, "V", sV, sV.n)
            sV.inc(v.reciprocal(p1, c2))
            mk.wait(v, "V", sV, sV.n)
            sV.inc(v.tensor_tensor(p2, e1c, p1, ALU.mult))
            mk.wait(v, "V", sV, sV.n)
            sV.inc(v.tensor_tensor(pd, p1, p2, ALU.subtract))
            sV.inc(v.tensor_tensor(dm, sm_st(ts), sm_s2(ts), ALU.not_equal))
            sV.inc(v.tensor_single_scalar(sm_w(ts), sm_st(ts), sm_m1(ts),
                                          ALU.is_equal))
            mk.wait(v, "V", sV, sV.n)
            sV.inc(v.tensor_scalar(sm_w(ts), sm_w(ts), pd, p2,
                                   ALU.mult, ALU.add))
            mk.wait(v, "V", sV, sV.n)
            sV.inc(v.tensor_tensor(sm_w(ts), sm_w(ts), dm, ALU.mult))
        blk.vector(f_vw)
    w_tok = sV.n

    for e in range(cfg.NE):
        vcur_b = sV.n
        scur_b = sS.n

        def f_tb(t, e=e, w_tok=w_tok, vcur_b=vcur_b):
            mk.wait(t, "T", sV, max(w_tok, vcur_b))
            for ts in range(4):
                i = t.transpose(mk.ps[5][0:1, ts * D:(ts + 1) * D],
                                sm[:, 96 + ts * 8 + e:97 + ts * 8 + e],
                                sb["ident_f"][:, :])
            sT.inc(i)
        blk.tensor(f_tb)
        tb = sT.n

        def f_sr(v, tb=tb):
            mk.wait(v, "V", sT, tb)
            mk.wait(v, "V", sV, sV.n)
            sV.inc(v.tensor_copy(out=sb["rowsb"][0:1, 0:TILE],
                                 in_=mk.ps[5][0:1, 0:TILE]))
        blk.vector(f_sr)
        srt = sV.n

        def f_tb2(t, srt=srt):
            mk.wait(t, "T", sV, srt)
            sT.inc(t.matmul(mk.ps[5][:, :], sb["ones_row"][0:1, 0:D],
                            sb["rowsb"][0:1, 0:TILE],
                            start=True, stop=True))
        blk.tensor(f_tb2)
        tb2 = sT.n

        def f_sb(v, tb2=tb2, e=e):
            mk.wait(v, "V", sT, tb2)
            mk.wait(v, "V", sV, sV.n)
            sV.inc(v.tensor_copy(out=ar(mk, 52 + e), in_=mk.ps[5][:, :]))
        blk.vector(f_sb)
    wbe8_tok = sV.n

    units1, _, _ = mk.wmeta["e1"]
    units3, _, _ = mk.wmeta["e3"]
    nmq1 = len(units1)
    for half in range(2):
        prev_g = (sV, sV.n)
        prev_u = (sV, sV.n)
        for ei in range(4):
            e = half * 4 + ei
            for mq in range(nmq1):
                ug = units1[mq]
                nm = (ug.ncols + D - 1) // D
                ld_g = ws.load("e1", e, ug)

                def f_tg_(t, ld=ld_g, u=ug, nm=nm, pf=prev_g, e=e, mq=mq):
                    first = e == 0 and mq == 0 and half == 0
                    if not first:
                        mk.wait(t, "T", sV, xh_tok)
                    mk.wait(t, "T", ld.sem, ld.tok)
                    if pf is not None:
                        mk.wait(t, "T", pf[0], pf[1])
                    last = None
                    for ki in range(cfg.KC):
                        if first and ki % 4 == 0:
                            mk.wait(t, "T", sV,
                                    xh_toks[min(ki + 3, cfg.KC - 1)])
                        for m in range(nm):
                            mc = min(D, u.ncols - m * D)
                            last = t.matmul(
                                mk.ps[m][:, :],
                                ld.buf[:, ki * u.ncols + m * D:ki * u.ncols + m * D + mc],
                                sb["xhat"][:, ki * TILE:(ki + 1) * TILE],
                                start=(ki == 0), stop=(ki == cfg.KC - 1))
                    sT.inc(last)
                blk.tensor(f_tg_)
                g_tok2 = sT.n
                ws.mark_consumed(ld_g.slot, sT, g_tok2)
                ld_u = ws.load("e2", e, ug)

                def f_tu_(t, ld=ld_u, u=ug, nm=nm, pf=prev_u):
                    mk.wait(t, "T", sV, wbe8_tok)
                    mk.wait(t, "T", ld.sem, ld.tok)
                    if pf is not None:
                        mk.wait(t, "T", pf[0], pf[1])
                    last = None
                    for ki in range(cfg.KC):
                        for m in range(nm):
                            mc = min(D, u.ncols - m * D)
                            last = t.matmul(
                                mk.ps[4 + m][:, :],
                                ld.buf[:, ki * u.ncols + m * D:ki * u.ncols + m * D + mc],
                                sb["xhat"][:, ki * TILE:(ki + 1) * TILE],
                                start=(ki == 0), stop=(ki == cfg.KC - 1))
                    sT.inc(last)
                blk.tensor(f_tu_)
                u_tok = sT.n
                ws.mark_consumed(ld_u.slot, sT, u_tok)

                def f_si(s, g_tok2=g_tok2, nm=nm, pu=prev_u):
                    mk.wait(s, "S", sT, g_tok2)
                    if pu is not None:
                        mk.wait(s, "S", pu[0], pu[1])
                    for m in range(nm):
                        i = s.activation(out=ar(mk, 44 + m), in_=mk.ps[m][:, :],
                                         func=AF.Silu)
                    sS.inc(i)
                blk.scalar(f_si)
                si_tok = sS.n
                prev_g = (sS, si_tok)

                def f_h(v, si_tok=si_tok, u_tok=u_tok, nm=nm, mq=mq, ei=ei,
                        e=e):
                    mk.wait(v, "V", sS, si_tok)
                    mk.wait(v, "V", sT, u_tok)
                    mk.wait(v, "V", sV, wbe8_tok)
                    for m in range(nm):
                        i = v.tensor_tensor(ar(mk, 48 + m), ar(mk, 44 + m),
                                            mk.ps[4 + m][:, :], ALU.mult)
                    sV.inc(i)
                    mk.wait(v, "V", sV, sV.n)
                    for m in range(nm):
                        fc = mq * 4 + m
                        i = v.tensor_tensor(ar(mk, ei * cfg.KM + fc),
                                            ar(mk, 48 + m), ar(mk, 52 + e),
                                            ALU.mult)
                    sV.inc(i)
                blk.vector(f_h)
                prev_u = (sV, sV.n)
        he_tok = sV.n

        prev_free = [None, None]
        for mq in range(cfg.KC // 4):
            psbase = 4 * (mq % 2)
            pf = prev_free[mq % 2]
            for ei in range(4):
                e = half * 4 + ei
                u = units3[mq]
                ld = ws.load("e3", e, u)

                def f_t(t, ld=ld, u=u, ei=ei, psbase=psbase, pf=pf):
                    mk.wait(t, "T", sV, he_tok)
                    mk.wait(t, "T", ld.sem, ld.tok)
                    if ei == 0 and pf is not None:
                        mk.wait(t, "T", pf[0], pf[1])
                    last = None
                    for ki in range(cfg.KM):
                        for m in range(4):
                            last = t.matmul(
                                mk.ps[psbase + m][:, :],
                                ld.buf[:, ki * TILE + m * D:ki * TILE + (m + 1) * D],
                                ar(mk, ei * cfg.KM + ki),
                                start=(ei == 0 and ki == 0),
                                stop=(ei == 3 and ki == cfg.KM - 1))
                    sT.inc(last)
                blk.tensor(f_t)
                ws.mark_consumed(ld.slot, sT, sT.n)
                if half == 1 and mq == cfg.KC // 4 - 1 and ei == 0:
                    mk.p6_prev_user = [(sT, sT.n)]
            ttok_ = sT.n

            if half == 0:
                def f_v(v, mq=mq, psbase=psbase, ttok_=ttok_):
                    mk.wait(v, "V", sT, ttok_)
                    mk.wait(v, "V", mk.sems["sX3st"], x3st)
                    for m in range(4):
                        kc = mq * 4 + m
                        i = v.tensor_copy(
                            out=sb["Xres"][:, kc * TILE:(kc + 1) * TILE],
                            in_=mk.ps[psbase + m][:, :])
                    sV.inc(i)
                blk.vector(f_v)
            else:
                def f_v(v, mq=mq, psbase=psbase, ttok_=ttok_):
                    mk.wait(v, "V", sT, ttok_)
                    for m in range(4):
                        kc = mq * 4 + m
                        x = sb["Xres"][:, kc * TILE:(kc + 1) * TILE]
                        i = v.tensor_tensor(x, mk.ps[psbase + m][:, :], x,
                                            ALU.add)
                    sV.inc(i)
                blk.vector(f_v)
            prev_free[mq % 2] = (sV, sV.n)
    y_tok = sV.n

    for kc in range(cfg.KC):
        vcur = sV.n

        def f_ld(sync, kc=kc, vcur=vcur):
            mk.wait(sync, "SY", mk.sems["sX3st"], x3st)
            if kc >= 1:
                mk.wait(sync, "SY", sV, vcur)
            mk.sem("sX3L").inc(sync.dma_start(
                out=sb["fstage"][:],
                in_=mk.dram["X3"].ap()[b, tq, :, kc, :]), 16)
        blk.sync(f_ld)
        l_tok = mk.sems["sX3L"].n

        def f_add(v, kc=kc, l_tok=l_tok):
            mk.wait(v, "V", mk.sems["sX3L"], l_tok)
            mk.wait(v, "V", sV, y_tok)
            x = sb["Xres"][:, kc * TILE:(kc + 1) * TILE]
            sV.inc(v.tensor_tensor(x, sb["fstage"][:], x, ALU.add))
        blk.vector(f_add)
    out_tok = sV.n

    def f_st(sc):
        mk.wait(sc, "SC", sV, out_tok)
        mk.sem("sOutSt").inc(sc.dma_start(
            out=mk.dram["OUT"].ap()[b, tq],
            in_=sb["Xres"][:].rearrange("p (k t) -> p k t", k=cfg.KC)), 16)
    blk.scalar(f_st)



def build_program(cfg, phases):
    nc = bass.Bass("TRN2", target_bir_lowering=False)
    stack = contextlib.ExitStack()
    mk = MK(nc, cfg, stack)
    fused = len(phases) > 2
    with stack:
        wnames = sorted({w for p in phases for w in PHASE_W.get(p, [])})
        declare_weights(mk, wnames, need_g1=("p6" in phases),
                        need_bias=("p4" in phases or "p6" in phases))
        used = set()
        for p in phases:
            used.update(PHASE_IN.get(p, []))
            used.update(PHASE_OUT.get(p, []))
        if "p6" in phases:
            used.add("X3")
        kinds = {}
        for n in used:
            inn = any(n in PHASE_IN.get(p, []) for p in phases)
            outn = any(n in PHASE_OUT.get(p, []) for p in phases)
            produced_here = outn
            needed_later = inn and not produced_here
            if n == "OUT":
                kinds[n] = "ExternalOutput"
            elif n == "Xin":
                kinds[n] = "ExternalInput"
            elif n == "X3":
                kinds[n] = "Internal"
            elif fused:
                kinds[n] = "Internal"
            elif needed_later:
                kinds[n] = "ExternalInput"
            else:
                kinds[n] = "ExternalOutput"
        declare_scratch(mk, sorted(used), kinds)
        declare_consts(mk)
        declare_sbuf(mk)
        mk.sbuf("identb", [D, D], BF16)
        mk.psum_init()
        mk._rope_last_t = {}
        mk._rope_ps_free = {}
        with nc.Block() as blk:
            mk.stg = StageRing(mk, blk, 56, 4)
            emit_startup(mk, blk, phases)
            cs_names = {"sSY", "sG", "sV", "sT"}
            for eng, who in (("tensor", "T"), ("vector", "V"),
                             ("scalar", "S"), ("gpsimd", "P"), ("sync", "SY")):
                getattr(blk, eng)(lambda e, who=who: mk.wait_all(e, who, cs_names))
            mk.conv_marks = {}

            def conv_stage(names_):
                todo = [n for n in names_ if n in mk.wmeta]
                if todo:
                    emit_p0(mk, blk, todo)
            conv_stage(["q0", "k0", "v0"])
            tiles = [(b, tq) for b in range(cfg.B) for tq in range(cfg.TQ)]
            if "p1" in phases:
                for b, tq in tiles:
                    xht, xh = emit_ln(mk, blk, b, tq, "Xin", xbf_off=40, sq_off=0)
                    emit_qkv(mk, blk, 0, b, tq, xht, xh)
                    if tq == cfg.TQ - 1:
                        emit_bound(mk, blk, 0, b)
            if "p2" in phases:
                emit_fence(mk, blk)
            conv_stage(["o0", "gu0", "dn0"])
            if "p2" in phases:
                for b in range(cfg.B):
                    emit_negB_col(mk, blk, 0, b)
                    for h in range(0, cfg.H, 2):
                        emit_attn2(mk, blk, 0, b, h)
            if "p3" in phases:
                emit_fence(mk, blk)
                for b, tq in tiles:
                    emit_p3_tile(mk, blk, b, tq)
            if "p4" in phases:
                emit_fence(mk, blk)
            conv_stage(["q1", "k1", "v1"])
            if "p4" in phases:
                emit_build_qe(mk, blk)
                for b, tq in tiles:
                    xht, xh = emit_ln(mk, blk, b, tq, "X2", xbf_off=40, sq_off=0)
                    emit_qkv(mk, blk, 1, b, tq, xht, xh)
                    if tq == cfg.TQ - 1:
                        emit_bound(mk, blk, 1, b)
            if "p5" in phases:
                emit_fence(mk, blk)
            conv_stage(["o1", "e1", "e2", "e3"])
            if "p5" in phases:
                for b in range(cfg.B):
                    emit_negB_col(mk, blk, 1, b)
                    for h in range(0, cfg.NKV, 2):
                        emit_attn2(mk, blk, 1, b, h)
            if "p6" in phases:
                emit_fence(mk, blk)
                for b, tq in tiles:
                    emit_p6_tile(mk, blk, b, tq)

            def f_fin(sync):
                mk.wait_all(sync, "SYF")
            blk.sync(f_fin)
        mk.check_promises()
    return nc, mk



def pack_weights(cfg, inputs):
    perm = rope_perm(cfg.E)
    permk = rope_perm(cfg.CKV)
    out = {
        "wf_q0": host_block_weight(inputs["L0_attn_wq"]),
        "wf_k0": host_block_weight(inputs["L0_attn_wk"]),
        "wf_v0": host_block_weight(inputs["L0_attn_wv"]),
        "wf_o0": host_block_weight(inputs["L0_attn_wo"]),
        "wf_gu0": host_block_weight(np.asarray(inputs["L0_ffn_w_gate_up"]).T),
        "wf_dn0": host_block_weight(np.asarray(inputs["L0_ffn_w_down"]).T),
        "wf_q1": host_block_weight(np.asarray(inputs["L1_attn_Wq"])[:, perm]),
        "wf_k1": host_block_weight(np.asarray(inputs["L1_attn_Wk"])[:, permk]),
        "wf_v1": host_block_weight(inputs["L1_attn_Wv"]),
        "wf_o1": host_block_weight(inputs["L1_attn_Wo"]),
        "wf_g1": host_block_weight(np.asarray(inputs["L1_ffn_Wg"]).T,
                                   colq=cfg.NE),
        "bias_bq": np.asarray(inputs["L1_attn_bq"], dtype=np.float32)[perm][None, :],
        "bias_bk": np.asarray(inputs["L1_attn_bk"], dtype=np.float32)[permk][None, :],
        "bias_bv": np.asarray(inputs["L1_attn_bv"], dtype=np.float32)[None, :],
        "bias_bo": np.asarray(inputs["L1_attn_bo"], dtype=np.float32)[None, :],
    }
    for nm, key in (("e1", "L1_ffn_W1"), ("e2", "L1_ffn_W2"),
                    ("e3", "L1_ffn_W3")):
        w = np.asarray(inputs[key])
        out["wf_" + nm] = np.concatenate(
            [host_block_weight(w[e].T) for e in range(cfg.NE)], axis=1)
    return out


_BUILD_CACHE = {}


def kernel(**inputs):
    cfg = make_cfg()
    key = "fused"
    if key not in _BUILD_CACHE:
        _BUILD_CACHE[key] = build_program(cfg, PHASES_ALL)
    nc, mk = _BUILD_CACHE[key]
    wmap = pack_weights(cfg, inputs)
    x = np.asarray(inputs["L0_attn_x"], dtype=np.float32)
    in_maps = []
    for core in range(NCORES):
        m = dict(wmap)
        m["Xin"] = host_block_x(x[cfg.B * core:cfg.B * (core + 1)])
        in_maps.append(m)
    res = run_bass_kernel_spmd(nc, in_maps, core_ids=list(range(NCORES)))
    outs = [host_unblock_x(r["OUT"]) for r in res.results]
    return np.concatenate(outs, axis=0)
